# version 02
# baseline (speedup 1.0000x reference)
import numpy as np
import concourse.bass as bass
import concourse.mybir as mybir
from concourse.bass_utils import run_bass_kernel_spmd
from contextlib import ExitStack

F32 = mybir.dt.float32
BF16 = mybir.dt.bfloat16
AF = mybir.ActivationFunctionType
ALU = mybir.AluOpType
AX = mybir.AxisListType.X

D = 2048
H = 5120
DI = 256
TOK = 2176
OWN0 = 64
OWN = 2048
K16 = 16
M40 = 40
TILES = [(0, 512), (512, 512), (1024, 512), (1536, 512), (2048, 128)]
NWT_FC = 80
NWT_PR = 80
LN_EPS = 1e-5

STREAMS = ("t", "v", "s", "g", "y")


class Sched:
    def __init__(self, nc, es):
        self.nc = nc
        self.ops = {e: [] for e in STREAMS}
        self.csem = {e: es.enter_context(nc.semaphore("c_" + e)) for e in STREAMS}
        self.dsem = {e: es.enter_context(nc.semaphore("d_" + e)) for e in ("s", "g", "y")}
        self.cnt = {("c", e): 0 for e in STREAMS}
        for e in ("s", "g", "y"):
            self.cnt[("d", e)] = 0
        self.waited = {e: {} for e in STREAMS}
        self.writer = {}
        self.readers = {}

    def _sem(self, key):
        kind, e = key
        return self.csem[e] if kind == "c" else self.dsem[e]

    def _wait(self, eng, mkey, val):
        if val <= 0:
            return
        if self.waited[eng].get(mkey, 0) >= val:
            return
        self.waited[eng][mkey] = val
        sem = self._sem(mkey)
        self.ops[eng].append(lambda e, sem=sem, val=val: e.wait_ge(sem, val))

    def _deps(self, eng, reads, writes):
        for r in reads:
            w = self.writer.get(r)
            if w:
                self._wait(eng, w[0], w[1])
        for wr in writes:
            w = self.writer.get(wr)
            if w:
                self._wait(eng, w[0], w[1])
            for rd in self.readers.get(wr, ()):
                self._wait(eng, rd[0], rd[1])

    def _record(self, my, reads, writes):
        for r in reads:
            self.readers.setdefault(r, []).append(my)
        for wr in writes:
            self.writer[wr] = my
            self.readers[wr] = []

    def op(self, eng, fn, reads=(), writes=()):
        self._deps(eng, reads, writes)
        key = ("c", eng)
        self.cnt[key] += 1
        sem, val = self.csem[eng], self.cnt[key]
        self.ops[eng].append(lambda e, fn=fn, sem=sem: fn(e).then_inc(sem, 1))
        self._record((key, val), reads, writes)

    def group(self, eng, fns, reads=(), writes=()):
        self._deps(eng, reads, writes)
        key = ("c", eng)
        self.cnt[key] += 1
        sem, val = self.csem[eng], self.cnt[key]
        for fn in fns[:-1]:
            self.ops[eng].append(lambda e, fn=fn: fn(e))
        self.ops[eng].append(lambda e, fn=fns[-1], sem=sem: fn(e).then_inc(sem, 1))
        self._record((key, val), reads, writes)

    def dma(self, eng, out_ap, in_ap, reads=(), writes=()):
        self._deps(eng, reads, writes)
        key = ("d", eng)
        self.cnt[key] += 1
        sem, val = self.dsem[eng], self.cnt[key] * 16
        self.ops[eng].append(
            lambda e, o=out_ap, i=in_ap, sem=sem: e.dma_start(out=o, in_=i).then_inc(sem, 16))
        self._record((key, val), reads, writes)


SUB = 99
def build(nc, upto=5):
    es = ExitStack()
    xT = nc.dram_tensor("xT", [D, TOK], F32, kind="ExternalInput")
    wfcT = nc.dram_tensor("wfcT", [D, H], F32, kind="ExternalInput")
    wprojT = nc.dram_tensor("wprojT", [H, D], F32, kind="ExternalInput")
    win = nc.dram_tensor("win", [D, DI], F32, kind="ExternalInput")
    wfwd = nc.dram_tensor("wfwd", [DI, 2 * DI], F32, kind="ExternalInput")
    wbwd = nc.dram_tensor("wbwd", [DI, 2 * DI], F32, kind="ExternalInput")
    wout = nc.dram_tensor("wout", [2 * DI, D], F32, kind="ExternalInput")
    gamT = nc.dram_tensor("gamT", [128, K16], F32, kind="ExternalInput")
    betT = nc.dram_tensor("betT", [128, K16], F32, kind="ExternalInput")
    yT = nc.dram_tensor("yT", [D, OWN], F32, kind="ExternalOutput")

    qfc = nc.dram_tensor("qfc", [D, H], BF16)
    qpr = nc.dram_tensor("qpr", [H, D], BF16)
    winq = nc.dram_tensor("winq", [D, DI], BF16)
    woutq = nc.dram_tensor("woutq", [2 * DI, D], BF16)
    y1s = nc.dram_tensor("y1s", [D, TOK], F32)

    fcf = wfcT.ap().rearrange("a b -> (a b)").rearrange("(n p f) -> n p f", p=128, f=1024)
    prf = wprojT.ap().rearrange("a b -> (a b)").rearrange("(n p f) -> n p f", p=128, f=1024)
    qfcf = qfc.ap().rearrange("a b -> (a b)").rearrange("(n p f) -> n p f", p=128, f=1024)
    qprf = qpr.ap().rearrange("a b -> (a b)").rearrange("(n p f) -> n p f", p=128, f=1024)
    qfck = qfc.ap().rearrange("(k p) h -> k p h", p=128)
    qprk = qpr.ap().rearrange("(k p) h -> k p h", p=128)
    winqk = winq.ap().rearrange("(k p) h -> k p h", p=128)
    woutqk = woutq.ap().rearrange("(k p) h -> k p h", p=128)

    sb = lambda n_, s, d: es.enter_context(nc.sbuf_tensor(n_, s, d))
    resid = sb("resid", [128, K16 * 512], F32)
    lnb = sb("lnb", [128, K16 * 512], BF16)
    hb = sb("hb", [128, M40 * 512], BF16)
    pool = sb("pool", [128, 8960], F32)
    ws = sb("ws", [128, 3 * 2560], BF16)
    zbuf = sb("zbuf", [128, 2 * TOK], BF16)
    hfb = sb("hfb", [128, 4 * TOK], BF16)
    dg = sb("dg", [128, 2 * 512], F32)
    qb = hfb[:, 4352:6400]
    mk = hfb[:, 6400:8448]
    wfb = sb("wfb", [128, 4 * 512], BF16)
    sq2 = sb("sq2", [128, 2 * 512], BF16)
    tch = sb("tch", [128, 512], F32)
    l1t = sb("l1t", [128, 512], F32)
    C1 = sb("C1", [128, 512], F32)
    C2 = sb("C2", [128, 512], F32)
    gam = sb("gam", [128, K16], F32)
    bet = sb("bet", [128, K16], F32)
    asum = sb("asum", [128, NWT_FC + NWT_PR], F32)
    musv = sb("musv", [1, TOK], F32)
    rssv = sb("rssv", [1, TOK], F32)
    sm = sb("sm", [1, 512], F32)
    sm2 = sb("sm2", [1, 512], F32)
    sm3 = sb("sm3", [1, 512], F32)
    onec = sb("onec", [128, 1], BF16)
    onecf = sb("onecf", [128, 1], F32)
    oner = sb("oner", [1, 128], F32)
    thrv = sb("thrv", [128, 6], F32)
    svals = sb("svals", [1, 8], F32)

    psA = es.enter_context(nc.psum_tensor("psA", [128, 512], F32))
    psB = es.enter_context(nc.psum_tensor("psB", [128, 512], F32))
    psS1 = es.enter_context(nc.psum_tensor("psS1", [1, 512], F32))
    psS2 = es.enter_context(nc.psum_tensor("psS2", [1, 512], F32))
    psC1 = es.enter_context(nc.psum_tensor("psC1", [128, 512], F32))
    psC2 = es.enter_context(nc.psum_tensor("psC2", [128, 512], F32))

    S = Sched(nc, es)
    block = es.enter_context(nc.Block())

    S.op("v", lambda e: e.memset(onec[:], 1.0), writes=["onec"])
    S.op("v", lambda e: e.memset(onecf[:], 1.0), writes=["onecf"])
    S.op("v", lambda e: e.memset(oner[:], 1.0), writes=["oner"])
    S.dma("y", gam[:], gamT[:], writes=["gam"])
    S.dma("y", bet[:], betT[:], writes=["bet"])

    def cast_small(src2d, k, dst_ap, dst_key, ncols):
        wa = pool[:, 0:ncols]
        S.dma("y", wa, src2d[k * 128:(k + 1) * 128, 0:ncols], writes=["wA"])
        S.op("s", lambda e, d=dst_ap, s_=wa: e.copy(d, s_), reads=["wA"], writes=[dst_key])

    for k in range(2):
        cast_small(wfwd.ap(), k, wfb[:, k * 512:(k + 1) * 512], "wfb", 512)
        cast_small(wbwd.ap(), k, wfb[:, 1024 + k * 512:1024 + (k + 1) * 512], "wfb", 512)
    for k in range(K16):
        wa = pool[:, 0:DI]
        S.dma("y", wa, win[k * 128:(k + 1) * 128, 0:DI], writes=["wA"])
        qv = qb[:, 0:DI]
        S.op("s", lambda e, d=qv, s_=wa: e.copy(d, s_), reads=["wA"], writes=["qb0"])
        S.dma("g", winqk[k], qv, reads=["qb0"], writes=["winq"])
    for k in range(4):
        wa = pool[:, 0:1024]
        wa2 = pool[:, 0:1024]
        for half in range(2):
            src = wout[k * 128:(k + 1) * 128, half * 1024:(half + 1) * 1024]
            S.dma("y", wa, src, writes=["wA"])
            qv = qb[:, 0:1024]
            S.op("s", lambda e, d=qv, s_=wa: e.copy(d, s_), reads=["wA"], writes=["qb0"])
            S.dma("g", woutqk[k][:, half * 1024:(half + 1) * 1024], qv,
                  reads=["qb0"], writes=["woutq"])

    for i in range(NWT_FC + NWT_PR if upto >= 1 else 0):
        src = fcf[i] if i < NWT_FC else prf[i - NWT_FC]
        buf_key = "wA" if i % 2 == 0 else "wB"
        buf = pool[:, 0:1024] if i % 2 == 0 else pool[:, 1024:2048]
        S.dma("y", buf, src, writes=[buf_key])
        eng = "v"
        S.op(eng, lambda e, o=asum[:, i:i + 1], i_=buf: e.reduce_sum(
            o, i_, axis=AX, apply_absolute_value=True),
            reads=[buf_key], writes=["asum"])
    if upto < 1:
        emit_none = True
    S.group("t", [lambda e: e.matmul(psS1[:, 0:NWT_FC + NWT_PR], onecf[:],
                                     asum[:, 0:NWT_FC + NWT_PR], start=True, stop=True)],
            reads=["asum", "onecf"], writes=["psS1"])
    S.op("v", lambda e: e.reduce_sum(svals[:, 0:1], psS1[:, 0:NWT_FC], axis=AX),
         reads=["psS1"], writes=["svals"])
    S.op("v", lambda e: e.reduce_sum(svals[:, 1:2], psS1[:, NWT_FC:NWT_FC + NWT_PR], axis=AX),
         reads=["psS1"], writes=["svals"])
    inv = 1.0 / float(D * H)
    S.op("v", lambda e: e.tensor_scalar(svals[:, 2:3], svals[:, 0:1], inv, None, ALU.mult),
         reads=["svals"], writes=["svals"])
    S.op("v", lambda e: e.tensor_scalar(svals[:, 3:4], svals[:, 1:2], inv, None, ALU.mult),
         reads=["svals"], writes=["svals"])
    S.op("v", lambda e: e.tensor_scalar(svals[:, 4:5], svals[:, 2:3], 0.7, None, ALU.mult),
         reads=["svals"], writes=["svals"])
    S.op("v", lambda e: e.tensor_scalar(svals[:, 5:6], svals[:, 3:4], 0.7, None, ALU.mult),
         reads=["svals"], writes=["svals"])
    S.op("v", lambda e: e.tensor_mul(svals[:, 6:7], svals[:, 2:3], svals[:, 2:3]),
         reads=["svals"], writes=["svals"])
    S.op("v", lambda e: e.tensor_mul(svals[:, 6:7], svals[:, 6:7], svals[:, 3:4]),
         reads=["svals"], writes=["svals"])
    S.op("v", lambda e: e.tensor_scalar(svals[:, 6:7], svals[:, 6:7], 0.5, None, ALU.mult),
         reads=["svals"], writes=["svals"])
    S.op("v", lambda e: e.tensor_copy(sm[:, 0:1], svals[:, 4:5]), reads=["svals"], writes=["sm"])
    S.op("v", lambda e: e.tensor_copy(sm[:, 1:2], svals[:, 5:6]), reads=["svals"], writes=["sm"])
    S.op("v", lambda e: e.tensor_copy(sm[:, 2:3], svals[:, 6:7]), reads=["svals"], writes=["sm"])
    S.op("v", lambda e: e.tensor_scalar(sm[:, 3:4], svals[:, 4:5], -1.0, None, ALU.mult),
         reads=["svals"], writes=["sm"])
    S.op("v", lambda e: e.tensor_scalar(sm[:, 4:5], svals[:, 5:6], -1.0, None, ALU.mult),
         reads=["svals"], writes=["sm"])
    S.group("t", [lambda e: e.matmul(psC1[:, 0:6], oner[:], sm[:, 0:6], start=True, stop=True)],
            reads=["sm", "oner"], writes=["psC1"])
    S.op("v", lambda e: e.tensor_copy(thrv[:], psC1[:, 0:6]), reads=["psC1"], writes=["thrv"])

    for i in range(NWT_FC + NWT_PR if upto >= 2 else 0):
        fc_part = i < NWT_FC
        src = fcf[i] if fc_part else prf[i - NWT_FC]
        dst = qfcf[i] if fc_part else qprf[i - NWT_FC]
        thr_col = thrv[:, 0:1] if fc_part else thrv[:, 1:2]
        nthr_col = thrv[:, 3:4] if fc_part else thrv[:, 4:5]
        bi = i % 2
        buf_key = "wA" if bi == 0 else "wB"
        buf = pool[:, bi * 1024:(bi + 1) * 1024]
        qv = qb[:, bi * 1024:(bi + 1) * 1024]
        mv = mk[:, bi * 1024:(bi + 1) * 1024]
        qk, mkk = f"qb{bi}", f"mk{bi}"
        S.dma("y", buf, src, writes=[buf_key])
        S.op("v", lambda e, o=qv, i_=buf, t_=thr_col: e.tensor_scalar(
            o, i_, t_, None, ALU.is_gt), reads=[buf_key, "thrv"], writes=[qk])
        S.op("g", lambda e, o=mv, i_=buf, t_=nthr_col: e.tensor_scalar(
            o, i_, t_, None, ALU.is_lt), reads=[buf_key, "thrv"], writes=[mkk])
        S.op("v", lambda e, o=qv, a=qv, b=mv: e.tensor_sub(o, a, b),
             reads=[qk, mkk], writes=[qk])
        S.dma("g", dst, qv, reads=[qk],
              writes=["qfc_scr" if fc_part else "qpr_scr"] if i in (NWT_FC - 1, NWT_FC + NWT_PR - 1) else [])

    ws_state = {"idx": 0}

    def next_slot():
        i = ws_state["idx"] % 3
        ws_state["idx"] += 1
        return ws[:, i * 2560:(i + 1) * 2560], f"ws{i}"

    def load_wset(scr_k, kcount, mcol, slot, skey, scr_key, width=128):
        for k0 in range(0, kcount, 4):
            kk = min(4, kcount - k0)
            src = scr_k[k0:k0 + kk, :, mcol:mcol + width].rearrange("kk p f -> p kk f")
            dst = slot[:, k0 * width:(k0 + kk) * width].rearrange("p (kk f) -> p kk f", kk=kk)
            S.dma("y", dst, src, reads=[scr_key], writes=[skey])

    def ln_smalls(nfeat, ts, mu_dst=None, rs_dst=None):
        iv = 1.0 / float(nfeat)
        S.op("v", lambda e: e.tensor_scalar(sm[:, 0:ts], psS1[:, 0:ts], iv, None, ALU.mult),
             reads=["psS1"], writes=["sm"])
        S.op("v", lambda e: e.tensor_scalar(sm2[:, 0:ts], psS2[:, 0:ts], iv, None, ALU.mult),
             reads=["psS2"], writes=["sm2"])
        S.op("v", lambda e: e.tensor_mul(sm3[:, 0:ts], sm[:, 0:ts], sm[:, 0:ts]),
             reads=["sm"], writes=["sm3"])
        S.op("v", lambda e: e.tensor_sub(sm2[:, 0:ts], sm2[:, 0:ts], sm3[:, 0:ts]),
             reads=["sm2", "sm3"], writes=["sm2"])
        S.op("v", lambda e: e.tensor_scalar(sm2[:, 0:ts], sm2[:, 0:ts], LN_EPS, None, ALU.add),
             reads=["sm2"], writes=["sm2"])
        S.op("s", lambda e: e.activation(sm3[:, 0:ts], sm2[:, 0:ts], AF.Ln),
             reads=["sm2"], writes=["sm3"])
        S.op("s", lambda e: e.activation(sm2[:, 0:ts], sm3[:, 0:ts], AF.Exp, scale=-0.5),
             reads=["sm3"], writes=["sm2"])
        if mu_dst is not None:
            S.op("v", lambda e: e.tensor_copy(musv[:, mu_dst[0]:mu_dst[1]], sm[:, 0:ts]),
                 reads=["sm"], writes=["musv"])
            S.op("v", lambda e: e.tensor_copy(rssv[:, mu_dst[0]:mu_dst[1]], sm2[:, 0:ts]),
                 reads=["sm2"], writes=["rssv"])
        S.op("v", lambda e: e.tensor_mul(sm[:, 0:ts], sm[:, 0:ts], sm2[:, 0:ts]),
             reads=["sm", "sm2"], writes=["sm"])
        S.op("v", lambda e: e.tensor_scalar(sm[:, 0:ts], sm[:, 0:ts], -1.0, None, ALU.mult),
             reads=["sm"], writes=["sm"])
        bcast(ts)

    def bcast(ts):
        rs = sm2[:, 0:ts]
        nm = sm[:, 0:ts]
        S.group("t", [lambda e: e.matmul(psC1[:, 0:ts], oner[:], rs, start=True, stop=True)],
                reads=["sm2", "oner"], writes=["psC1"])
        S.group("t", [lambda e: e.matmul(psC2[:, 0:ts], oner[:], nm, start=True, stop=True)],
                reads=["sm", "oner"], writes=["psC2"])
        S.op("s", lambda e: e.copy(C1[:, 0:ts], psC1[:, 0:ts]), reads=["psC1"], writes=["C1"])
        S.op("s", lambda e: e.copy(C2[:, 0:ts], psC2[:, 0:ts]), reads=["psC2"], writes=["C2"])

    def stats(buf, bkeys, ts, cast_from=None, from_keys=None):
        for k in range(K16):
            bap, bkey = buf(k)
            if cast_from is not None:
                sap, skey = cast_from(k)
                S.op("s", lambda e, o=bap, i_=sap: e.copy(o, i_), reads=[skey], writes=[bkey])
            sqv = sq2[:, (k % 2) * 512:(k % 2) * 512 + ts]
            sqk = f"sq{k % 2}"
            S.op("g", lambda e, o=sqv, a=bap: e.tensor_mul(o, a, a), reads=[bkey], writes=[sqk])
            S.group("t", [lambda e, r=bap, k=k: e.matmul(
                psS1[:, 0:ts], onec[:], r, start=(k == 0), stop=(k == K16 - 1))],
                reads=[bkey, "onec"], writes=["psS1"])
            S.group("t", [lambda e, r=sqv, k=k: e.matmul(
                psS2[:, 0:ts], onec[:], r, start=(k == 0), stop=(k == K16 - 1))],
                reads=[sqk, "onec"], writes=["psS2"])

    def normalize(src, skeys, dst, dkeys, ts, gamma_beta=False):
        for k in range(K16):
            sap, skey = src(k)
            dap, dkey = dst(k)
            S.op("v", lambda e, o=tch[:, 0:ts], a=sap: e.tensor_mul(o, a, C1[:, 0:ts]),
                 reads=[skey, "C1"], writes=["tch"])
            if gamma_beta:
                S.op("g", lambda e, o=l1t[:, 0:ts]: e.tensor_add(o, tch[:, 0:ts], C2[:, 0:ts]),
                     reads=["tch", "C2"], writes=["l1t"])
                S.op("v", lambda e, o=dap, k=k: e.tensor_scalar(
                    o, l1t[:, 0:ts], gam[:, k:k + 1], bet[:, k:k + 1], ALU.mult, ALU.add),
                    reads=["l1t", "gam", "bet"], writes=[dkey])
            else:
                S.op("g", lambda e, o=dap: e.tensor_add(o, tch[:, 0:ts], C2[:, 0:ts]),
                     reads=["tch", "C2"], writes=[dkey])

    def ffn(t0, ts, src_resid_key, extra_first_writes=(), do_proj=True):
        for m in range(M40):
            slot, skey = next_slot()
            load_wset(qfck, K16, m * 128, slot, skey, "qfc_scr")
            ps, pk = (psA, "psA") if m % 2 == 0 else (psB, "psB")
            fns = [lambda e, k=k, sl=slot, p=ps, t0=t0, ts=ts: e.matmul(
                p[:, 0:ts], sl[:, k * 128:(k + 1) * 128],
                lnb[:, k * 512:k * 512 + ts], start=(k == 0), stop=(k == K16 - 1))
                for k in range(K16)]
            S.group("t", fns, reads=[skey] + [f"lnb{k}" for k in range(K16)], writes=[pk])
            sqv = sq2[:, (m % 2) * 512:(m % 2) * 512 + ts]
            sqk = f"sq{m % 2}"
            S.op("s", lambda e, o=sqv, p=ps: e.activation(o, p[:, 0:ts], AF.Relu),
                 reads=[pk], writes=[sqk])
            S.op("g", lambda e, o=hb[:, m * 512:m * 512 + ts], a=sqv: e.tensor_mul(o, a, a),
                 reads=[sqk], writes=[f"hb{m}"])
        for m2 in range(K16 if do_proj else 0):
            slotA, skA = next_slot()
            slotB, skB = next_slot()
            load_wset(qprk, 20, m2 * 128, slotA, skA, "qpr_scr")
            load_wset(qprk[20:40], 20, m2 * 128, slotB, skB, "qpr_scr")
            ps, pk = (psA, "psA") if m2 % 2 == 0 else (psB, "psB")
            fns = []
            for k in range(M40):
                sl = slotA if k < 20 else slotB
                kk = k if k < 20 else k - 20
                fns.append(lambda e, k=k, kk=kk, sl=sl, p=ps: e.matmul(
                    p[:, 0:ts], sl[:, kk * 128:(kk + 1) * 128],
                    hb[:, k * 512:k * 512 + ts], start=(k == 0), stop=(k == M40 - 1)))
            S.group("t", fns, reads=[skA, skB] + [f"hb{k}" for k in range(M40)], writes=[pk])
            rv = resid[:, m2 * 512:m2 * 512 + ts]
            S.op("v", lambda e, o=rv, p=ps: e.scalar_tensor_tensor(
                o, p[:, 0:ts], thrv[:, 2:3], o, ALU.mult, ALU.add),
                reads=[pk, "thrv", f"res{m2}"], writes=[f"res{m2}"])

    def emit_phase1():
        for ti, (t0, ts) in enumerate(TILES):
            tile1(ti, t0, ts)

    def tile1(ti, t0, ts):
        if True:
            for k in range(K16):
                S.dma("s", resid[:, k * 512:k * 512 + ts], xT[k * 128:(k + 1) * 128, t0:t0 + ts],
                      writes=[f"res{k}"])
            stats(lambda k: (lnb[:, k * 512:k * 512 + ts], f"lnb{k}"), None, ts,
                  cast_from=lambda k: (resid[:, k * 512:k * 512 + ts], f"res{k}"))
            if SUB < 2:
                return
            ln_smalls(D, ts)
            normalize(lambda k: (resid[:, k * 512:k * 512 + ts], f"res{k}"), None,
                      lambda k: (lnb[:, k * 512:k * 512 + ts], f"lnb{k}"), None, ts)
            if SUB < 3:
                return
            ffn(t0, ts, "res", do_proj=(SUB >= 4))
            if SUB < 5:
                return
            for k in range(K16):
                S.dma("s", y1s[k * 128:(k + 1) * 128, t0:t0 + ts],
                      resid[:, k * 512:k * 512 + ts], reads=[f"res{k}"])
            if SUB < 6:
                return
            stats(lambda k: (lnb[:, k * 512:k * 512 + ts], f"lnb{k}"), None, ts,
                  cast_from=lambda k: (resid[:, k * 512:k * 512 + ts], f"res{k}"))
            ln_smalls(D, ts, mu_dst=(t0, t0 + ts))
            if SUB < 7:
                return
            normalize(lambda k: (resid[:, k * 512:k * 512 + ts], f"res{k}"), None,
                      lambda k: (lnb[:, k * 512:k * 512 + ts], f"lnb{k}"), None, ts)
            if SUB < 8:
                return
            for m in range(2):
                slot, skey = next_slot()
                load_wset(winqk, K16, m * 128, slot, skey, "winq")
                if SUB < 9:
                    continue
                ps, pk = (psA, "psA") if m == 0 else (psB, "psB")
                fns = [lambda e, k=k, sl=slot, p=ps: e.matmul(
                    p[:, 0:ts], sl[:, k * 128:(k + 1) * 128],
                    lnb[:, k * 512:k * 512 + ts], start=(k == 0), stop=(k == K16 - 1))
                    for k in range(K16)]
                S.group("t", fns, reads=[skey] + [f"lnb{k}" for k in range(K16)], writes=[pk])
                if SUB < 10:
                    continue
                S.op("s", lambda e, o=zbuf[:, m * TOK + t0:m * TOK + t0 + ts], p=ps:
                     e.copy(o, p[:, 0:ts]), reads=[pk], writes=[f"zb{m}"])
                if SUB < 11:
                    continue
                S.op("s", lambda e, o=pool[:, 4608 + m * TOK + t0:4608 + m * TOK + t0 + ts],
                     p=ps: e.copy(o, p[:, 0:ts]), reads=[pk], writes=[f"zf{m}"])

    def emit_phase2():
        for d, order in ((1, list(reversed(range(len(TILES))))), (0, list(range(len(TILES))))):
            for c in range(2):
                for ti in order:
                    t0, ts = TILES[ti]
                    scan_tile(d, c, ti, t0, ts, order)

    def scan_tile(d, c, ti, t0, ts, order):
        if True:
            if True:
                if True:
                    fns = [lambda e, k2=k2: e.matmul(
                        psA[:, 0:ts], wfb[:, d * 1024 + k2 * 512 + c * 128:
                                          d * 1024 + k2 * 512 + c * 128 + 128],
                        zbuf[:, k2 * TOK + t0:k2 * TOK + t0 + ts],
                        start=(k2 == 0), stop=(k2 == 1)) for k2 in range(2)]
                    S.group("t", fns, reads=["wfb", "zb0", "zb1"], writes=["psA"])
                    S.op("s", lambda e: e.activation(dg[:, 0:ts], psA[:, 0:ts], AF.Sigmoid),
                         reads=["psA"], writes=["dgd"])
                    fns = [lambda e, k2=k2: e.matmul(
                        psB[:, 0:ts], wfb[:, d * 1024 + k2 * 512 + 256 + c * 128:
                                          d * 1024 + k2 * 512 + 256 + c * 128 + 128],
                        zbuf[:, k2 * TOK + t0:k2 * TOK + t0 + ts],
                        start=(k2 == 0), stop=(k2 == 1)) for k2 in range(2)]
                    S.group("t", fns, reads=["wfb", "zb0", "zb1"], writes=["psB"])
                    S.op("v", lambda e: e.tensor_mul(
                        dg[:, 512:512 + ts], psB[:, 0:ts],
                        pool[:, 4608 + c * TOK + t0:4608 + c * TOK + t0 + ts]),
                        reads=["psB", f"zf{c}"], writes=["dgg"])
                    hk = f"hsc{c}"
                    hview = pool[:, c * TOK + t0:c * TOK + t0 + ts]
                    d0 = dg[:, 0:ts]
                    d1 = dg[:, 512:512 + ts]
                    extra_w = ["wA", "wB"] if (ti == order[0] and c == 0) else []
                    if d == 0:
                        init = 0.0 if ti == 0 else pool[:, c * TOK + t0 - 1:c * TOK + t0]
                        S.op("v", lambda e, o=hview, a=d0, b=d1, i_=init: e.tensor_tensor_scan(
                            o, a, b, i_, ALU.mult, ALU.add),
                            reads=["dgd", "dgg", hk], writes=[hk] + extra_w)
                    else:
                        last = ti == len(TILES) - 1
                        init = 0.0 if last else pool[:, c * TOK + t0 + ts:c * TOK + t0 + ts + 1]
                        S.op("v", lambda e, o=hview[:, ::-1], a=d0[:, ::-1], b=d1[:, ::-1],
                             i_=init: e.tensor_tensor_scan(o, a, b, i_, ALU.mult, ALU.add),
                            reads=["dgd", "dgg", hk], writes=[hk] + extra_w)
                    hcol = (c if d == 0 else 2 + c) * TOK + t0
                    wtags = [f"hfb{c if d == 0 else 2 + c}"]
                    if d == 1:
                        wtags += ["qb0", "qb1", "mk0", "mk1"]
                    S.op("s", lambda e, o=hfb[:, hcol:hcol + ts], i_=hview: e.copy(o, i_),
                         reads=[hk], writes=wtags)

    def emit_phase3():
        for ti, (t0, ts) in enumerate(TILES):
            tile3(ti, t0, ts)

    def tile3(ti, t0, ts):
        if True:
            for k in range(K16):
                S.dma("s", resid[:, k * 512:k * 512 + ts], y1s[k * 128:(k + 1) * 128, t0:t0 + ts],
                      writes=[f"res{k}"])
            S.op("v", lambda e: e.tensor_copy(sm2[:, 0:ts], rssv[:, t0:t0 + ts]),
                 reads=["rssv"], writes=["sm2"])
            S.op("v", lambda e: e.tensor_mul(sm[:, 0:ts], musv[:, t0:t0 + ts], sm2[:, 0:ts]),
                 reads=["musv", "sm2"], writes=["sm"])
            S.op("v", lambda e: e.tensor_scalar(sm[:, 0:ts], sm[:, 0:ts], -1.0, None, ALU.mult),
                 reads=["sm"], writes=["sm"])
            bcast(ts)
            for m in range(K16):
                slot, skey = next_slot()
                load_wset(woutqk, 4, m * 128, slot, skey, "woutq")
                ps, pk = (psA, "psA") if m % 2 == 0 else (psB, "psB")
                fns = [lambda e, k=k, sl=slot, p=ps: e.matmul(
                    p[:, 0:ts], sl[:, k * 128:(k + 1) * 128],
                    hfb[:, k * TOK + t0:k * TOK + t0 + ts], start=(k == 0), stop=(k == 3))
                    for k in range(4)]
                S.group("t", fns, reads=[skey, "hfb0", "hfb1", "hfb2", "hfb3"], writes=[pk])
                rv = resid[:, m * 512:m * 512 + ts]
                S.op("v", lambda e, a=rv: e.tensor_mul(tch[:, 0:ts], a, C1[:, 0:ts]),
                     reads=[f"res{m}", "C1"], writes=["tch"])
                S.op("g", lambda e: e.tensor_add(l1t[:, 0:ts], tch[:, 0:ts], C2[:, 0:ts]),
                     reads=["tch", "C2"], writes=["l1t"])
                extra = ["zf0", "zf1", "hsc0", "hsc1"] if m == 0 else []
                S.op("v", lambda e, o=pool[:, m * 512:m * 512 + ts], p=ps: e.tensor_add(
                    o, p[:, 0:ts], l1t[:, 0:ts]),
                    reads=[pk, "l1t"], writes=[f"R2_{m}"] + extra)
            stats(lambda k: (lnb[:, k * 512:k * 512 + ts], f"lnb{k}"), None, ts,
                  cast_from=lambda k: (pool[:, k * 512:k * 512 + ts], f"R2_{k}"))
            ln_smalls(D, ts)
            for k in range(K16):
                pv = pool[:, k * 512:k * 512 + ts]
                S.op("v", lambda e, a=pv: e.tensor_mul(tch[:, 0:ts], a, C1[:, 0:ts]),
                     reads=[f"R2_{k}", "C1"], writes=["tch"])
                S.op("g", lambda e: e.tensor_add(l1t[:, 0:ts], tch[:, 0:ts], C2[:, 0:ts]),
                     reads=["tch", "C2"], writes=["l1t"])
                S.op("v", lambda e, k=k: e.tensor_scalar(
                    l1t[:, 0:ts], l1t[:, 0:ts], gam[:, k:k + 1], bet[:, k:k + 1],
                    ALU.mult, ALU.add), reads=["l1t", "gam", "bet"], writes=["l1t"])
                rv = resid[:, k * 512:k * 512 + ts]
                S.op("v", lambda e, o=rv: e.tensor_add(o, o, l1t[:, 0:ts]),
                     reads=[f"res{k}", "l1t"], writes=[f"res{k}"])
            stats(lambda k: (lnb[:, k * 512:k * 512 + ts], f"lnb{k}"), None, ts,
                  cast_from=lambda k: (resid[:, k * 512:k * 512 + ts], f"res{k}"))
            ln_smalls(D, ts)
            normalize(lambda k: (resid[:, k * 512:k * 512 + ts], f"res{k}"), None,
                      lambda k: (lnb[:, k * 512:k * 512 + ts], f"lnb{k}"), None, ts)
            ffn(t0, ts, "res")
            o0 = max(t0, OWN0)
            o1 = min(t0 + ts, OWN0 + OWN)
            if o1 > o0:
                for k in range(K16):
                    S.dma("s", yT[k * 128:(k + 1) * 128, o0 - OWN0:o1 - OWN0],
                          resid[:, k * 512 + (o0 - t0):k * 512 + (o1 - t0)],
                          reads=[f"res{k}"])

    if upto >= 3:
        emit_phase1()
    if upto >= 4:
        emit_phase2()
    if upto >= 5:
        emit_phase3()

    total = S.cnt[("d", "s")] * 16
    S.ops["s"].append(lambda e, sem=S.dsem["s"], v=total: e.wait_ge(sem, v))
    tg = S.cnt[("d", "g")] * 16
    S.ops["g"].append(lambda e, sem=S.dsem["g"], v=tg: e.wait_ge(sem, v))
    ty = S.cnt[("d", "y")] * 16
    S.ops["y"].append(lambda e, sem=S.dsem["y"], v=ty: e.wait_ge(sem, v))

    with es:
        @block.sync
        def _(e):
            for f in S.ops["y"]:
                f(e)

        @block.vector
        def _(e):
            for f in S.ops["v"]:
                f(e)

        @block.scalar
        def _(e):
            for f in S.ops["s"]:
                f(e)

        @block.gpsimd
        def _(e):
            for f in S.ops["g"]:
                f(e)

        @block.tensor
        def _(e):
            for f in S.ops["t"]:
                f(e)

    return nc


_NC = None


def kernel(L0_attn_x, L0_attn_W_in, L0_attn_W_fwd, L0_attn_W_bwd, L0_attn_W_out,
           L0_attn_gamma, L0_attn_beta, L0_ffn_w_fc, L0_ffn_w_proj):
    global _NC
    x = np.asarray(L0_attn_x, dtype=np.float32)
    B, L, Dm = x.shape
    wfcT = np.ascontiguousarray(np.asarray(L0_ffn_w_fc, np.float32).T)
    wprojT = np.ascontiguousarray(np.asarray(L0_ffn_w_proj, np.float32).T)
    win = np.ascontiguousarray(np.asarray(L0_attn_W_in, np.float32))
    wfwd = np.ascontiguousarray(np.asarray(L0_attn_W_fwd, np.float32))
    wbwd = np.ascontiguousarray(np.asarray(L0_attn_W_bwd, np.float32))
    wout = np.ascontiguousarray(np.asarray(L0_attn_W_out, np.float32))
    gamT = np.ascontiguousarray(np.asarray(L0_attn_gamma, np.float32).reshape(K16, 128).T)
    betT = np.ascontiguousarray(np.asarray(L0_attn_beta, np.float32).reshape(K16, 128).T)

    if _NC is None:
        nc = bass.Bass("TRN2", target_bir_lowering=False)
        _NC = build(nc)

    in_maps = []
    for c in range(8):
        b, s = c // 2, c % 2
        pad = np.zeros((TOK, Dm), np.float32)
        lo = s * OWN - OWN0
        hi = s * OWN + OWN + OWN0
        clo, chi = max(lo, 0), min(hi, L)
        pad[clo - lo:chi - lo, :] = x[b, clo:chi, :]
        xT = np.ascontiguousarray(pad.T)
        in_maps.append(dict(xT=xT, wfcT=wfcT, wprojT=wprojT, win=win, wfwd=wfwd,
                            wbwd=wbwd, wout=wout, gamT=gamT, betT=betT))

    res = run_bass_kernel_spmd(_NC, in_maps, core_ids=list(range(8)))
    out = np.empty((B, L, Dm), np.float32)
    for c in range(8):
        b, s = c // 2, c % 2
        out[b, s * OWN:(s + 1) * OWN, :] = np.asarray(res.results[c]["yT"]).T
    return out


# version 05
# speedup vs baseline: 1.2495x; 1.2495x over previous
import numpy as np
import concourse.bass as bass
import concourse.mybir as mybir
from concourse.bass_utils import run_bass_kernel_spmd
from contextlib import ExitStack

F32 = mybir.dt.float32
BF16 = mybir.dt.bfloat16
AF = mybir.ActivationFunctionType
ALU = mybir.AluOpType
AX = mybir.AxisListType.X

D = 2048
H = 5120
DI = 256
TOK = 2176
OWN0 = 64
OWN = 2048
K16 = 16
M40 = 40
TILES = [(0, 512), (512, 512), (1024, 512), (1536, 512), (2048, 128)]
NWT_FC = 80
NWT_PR = 80
LN_EPS = 1e-5

STREAMS = ("t", "v", "s", "g", "y")


class Sched:
    def __init__(self, nc, es):
        self.nc = nc
        self.es = es
        self.ops = {e: [] for e in STREAMS}
        self.csem = {e: es.enter_context(nc.semaphore("c_" + e)) for e in STREAMS}
        self.fsem = {}
        self.cnt = {("c", e): 0 for e in STREAMS}
        self.waited = {e: {} for e in STREAMS}
        self.writer = {}
        self.readers = {}

    def _sem(self, key):
        kind, e = key
        return self.csem[e] if kind == "c" else self.fsem[e]

    def _wait(self, eng, mkey, val):
        if val <= 0:
            return
        if self.waited[eng].get(mkey, 0) >= val:
            return
        self.waited[eng][mkey] = val
        sem = self._sem(mkey)
        self.ops[eng].append(lambda e, sem=sem, val=val: e.wait_ge(sem, val))

    def _deps(self, eng, reads, writes):
        for r in reads:
            w = self.writer.get(r)
            if w:
                self._wait(eng, w[0], w[1])
        for wr in writes:
            w = self.writer.get(wr)
            if w:
                self._wait(eng, w[0], w[1])
            for rd in self.readers.get(wr, ()):
                self._wait(eng, rd[0], rd[1])

    def _record(self, my, reads, writes):
        for r in reads:
            self.readers.setdefault(r, []).append(my)
        for wr in writes:
            self.writer[wr] = my
            self.readers[wr] = []

    def op(self, eng, fn, reads=(), writes=()):
        self._deps(eng, reads, writes)
        key = ("c", eng)
        self.cnt[key] += 1
        sem, val = self.csem[eng], self.cnt[key]
        self.ops[eng].append(lambda e, fn=fn, sem=sem: fn(e).then_inc(sem, 1))
        self._record((key, val), reads, writes)

    def group(self, eng, fns, reads=(), writes=()):
        self._deps(eng, reads, writes)
        key = ("c", eng)
        self.cnt[key] += 1
        sem, val = self.csem[eng], self.cnt[key]
        for fn in fns[:-1]:
            self.ops[eng].append(lambda e, fn=fn: fn(e))
        self.ops[eng].append(lambda e, fn=fns[-1], sem=sem: fn(e).then_inc(sem, 1))
        self._record((key, val), reads, writes)

    def dma(self, eng, out_ap, in_ap, reads=(), writes=(), fam="misc"):
        self._deps(eng, reads, writes)
        if fam not in self.fsem:
            self.fsem[fam] = self.es.enter_context(self.nc.semaphore("f_" + fam))
            self.cnt[("f", fam)] = 0
        key = ("f", fam)
        self.cnt[key] += 1
        sem, val = self.fsem[fam], self.cnt[key] * 16
        self.ops[eng].append(
            lambda e, o=out_ap, i=in_ap, sem=sem: e.dma_start(out=o, in_=i).then_inc(sem, 16))
        self._record((key, val), reads, writes)


SUB = 99
def build(nc, upto=5):
    es = ExitStack()
    xT = nc.dram_tensor("xT", [D, TOK], F32, kind="ExternalInput")
    wfcT = nc.dram_tensor("wfcT", [D, H], F32, kind="ExternalInput")
    wprojT = nc.dram_tensor("wprojT", [H, D], F32, kind="ExternalInput")
    win = nc.dram_tensor("win", [D, DI], F32, kind="ExternalInput")
    wfwd = nc.dram_tensor("wfwd", [DI, 2 * DI], F32, kind="ExternalInput")
    wbwd = nc.dram_tensor("wbwd", [DI, 2 * DI], F32, kind="ExternalInput")
    wout = nc.dram_tensor("wout", [2 * DI, D], F32, kind="ExternalInput")
    gamT = nc.dram_tensor("gamT", [128, K16], F32, kind="ExternalInput")
    betT = nc.dram_tensor("betT", [128, K16], F32, kind="ExternalInput")
    yT = nc.dram_tensor("yT", [D, OWN], F32, kind="ExternalOutput")

    qfc = nc.dram_tensor("qfc", [D, H], BF16)
    qpr = nc.dram_tensor("qpr", [H, D], BF16)
    winq = nc.dram_tensor("winq", [D, DI], BF16)
    woutq = nc.dram_tensor("woutq", [2 * DI, D], BF16)
    y1s = nc.dram_tensor("y1s", [D, TOK], F32)

    fcf = wfcT.ap().rearrange("a b -> (a b)").rearrange("(n p f) -> n p f", p=128, f=1024)
    prf = wprojT.ap().rearrange("a b -> (a b)").rearrange("(n p f) -> n p f", p=128, f=1024)
    qfcf = qfc.ap().rearrange("a b -> (a b)").rearrange("(n p f) -> n p f", p=128, f=1024)
    qprf = qpr.ap().rearrange("a b -> (a b)").rearrange("(n p f) -> n p f", p=128, f=1024)
    qfck = qfc.ap().rearrange("(k p) h -> k p h", p=128)
    qprk = qpr.ap().rearrange("(k p) h -> k p h", p=128)
    winqk = winq.ap().rearrange("(k p) h -> k p h", p=128)
    woutqk = woutq.ap().rearrange("(k p) h -> k p h", p=128)

    sb = lambda n_, s, d: es.enter_context(nc.sbuf_tensor(n_, s, d))
    resid = sb("resid", [128, K16 * 512], F32)
    lnb = sb("lnb", [128, K16 * 512], BF16)
    hb = sb("hb", [128, M40 * 512], BF16)
    pool = sb("pool", [128, 8960], F32)
    ws = sb("ws", [128, 3 * 2560], BF16)
    zbuf = sb("zbuf", [128, 2 * TOK], BF16)
    hfb = sb("hfb", [128, 4 * TOK], BF16)
    dg = sb("dg", [128, 2 * 512], F32)
    qb = hfb[:, 4352:6400]
    mk = hfb[:, 6400:8448]
    wfb = sb("wfb", [128, 4 * 512], BF16)
    sq2 = sb("sq2", [128, 2 * 512], BF16)
    tch = sb("tch", [128, 512], F32)
    l1t = sb("l1t", [128, 512], F32)
    C1 = sb("C1", [128, 512], F32)
    C2 = sb("C2", [128, 512], F32)
    gam = sb("gam", [128, K16], F32)
    bet = sb("bet", [128, K16], F32)
    asum = sb("asum", [128, NWT_FC + NWT_PR], F32)
    musv = sb("musv", [1, TOK], F32)
    rssv = sb("rssv", [1, TOK], F32)
    sm = sb("sm", [1, 512], F32)
    sm2 = sb("sm2", [1, 512], F32)
    sm3 = sb("sm3", [1, 512], F32)
    onec = sb("onec", [128, 1], BF16)
    onecf = sb("onecf", [128, 1], F32)
    oner = sb("oner", [1, 128], F32)
    thrv = sb("thrv", [128, 6], F32)
    svals = sb("svals", [1, 8], F32)

    psA = es.enter_context(nc.psum_tensor("psA", [128, 512], F32))
    psB = es.enter_context(nc.psum_tensor("psB", [128, 512], F32))
    psS1 = es.enter_context(nc.psum_tensor("psS1", [1, 512], F32))
    psS2 = es.enter_context(nc.psum_tensor("psS2", [1, 512], F32))
    psC = es.enter_context(nc.psum_tensor("psC", [128, 512], F32))
    psD = es.enter_context(nc.psum_tensor("psD", [128, 512], F32))
    psC1 = es.enter_context(nc.psum_tensor("psC1", [128, 512], F32))
    psC2 = es.enter_context(nc.psum_tensor("psC2", [128, 512], F32))

    psG = [(psA, "psA"), (psB, "psB"), (psC, "psC"), (psD, "psD")]
    S = Sched(nc, es)
    block = es.enter_context(nc.Block())

    S.op("v", lambda e: e.memset(onec[:], 1.0), writes=["onec"])
    S.op("v", lambda e: e.memset(onecf[:], 1.0), writes=["onecf"])
    S.op("v", lambda e: e.memset(oner[:], 1.0), writes=["oner"])
    S.dma("y", gam[:], gamT[:], writes=["gam"], fam="gld")
    S.dma("y", bet[:], betT[:], writes=["bet"], fam="bld")

    def cast_small(src2d, k, dst_ap, dst_key, ncols):
        wa = pool[:, 0:ncols]
        S.dma("y", wa, src2d[k * 128:(k + 1) * 128, 0:ncols], writes=["wA"], fam="wa")
        S.op("s", lambda e, d=dst_ap, s_=wa: e.copy(d, s_), reads=["wA"], writes=[dst_key])

    for k in range(2):
        cast_small(wfwd.ap(), k, wfb[:, k * 512:(k + 1) * 512], "wfb", 512)
        cast_small(wbwd.ap(), k, wfb[:, 1024 + k * 512:1024 + (k + 1) * 512], "wfb", 512)
    for k in range(K16):
        wa = pool[:, 0:DI]
        S.dma("y", wa, win[k * 128:(k + 1) * 128, 0:DI], writes=["wA"], fam="wa")
        qv = qb[:, 0:DI]
        S.op("s", lambda e, d=qv, s_=wa: e.copy(d, s_), reads=["wA"], writes=["qb0"])
        S.dma("s", winqk[k], qv, reads=["qb0"], writes=["winq"], fam="winw")
    for k in range(4):
        wa = pool[:, 0:1024]
        wa2 = pool[:, 0:1024]
        for half in range(2):
            src = wout[k * 128:(k + 1) * 128, half * 1024:(half + 1) * 1024]
            S.dma("y", wa, src, writes=["wA"], fam="wa")
            qv = qb[:, 0:1024]
            S.op("s", lambda e, d=qv, s_=wa: e.copy(d, s_), reads=["wA"], writes=["qb0"])
            S.dma("s", woutqk[k][:, half * 1024:(half + 1) * 1024], qv,
                  reads=["qb0"], writes=["woutq"], fam="woutw")

    for i in range(NWT_FC + NWT_PR if upto >= 1 else 0):
        src = fcf[i] if i < NWT_FC else prf[i - NWT_FC]
        buf_key = "wA" if i % 2 == 0 else "wB"
        buf = pool[:, 0:1024] if i % 2 == 0 else pool[:, 1024:2048]
        S.dma("y", buf, src, writes=[buf_key], fam="wa" if i % 2 == 0 else "wb")
        eng = "v"
        S.op(eng, lambda e, o=asum[:, i:i + 1], i_=buf: e.reduce_sum(
            o, i_, axis=AX, apply_absolute_value=True),
            reads=[buf_key], writes=["asum"])
    if upto < 1:
        emit_none = True
    S.group("t", [lambda e: e.matmul(psS1[:, 0:NWT_FC + NWT_PR], onecf[:],
                                     asum[:, 0:NWT_FC + NWT_PR], start=True, stop=True)],
            reads=["asum", "onecf"], writes=["psS1"])
    S.op("v", lambda e: e.reduce_sum(svals[:, 0:1], psS1[:, 0:NWT_FC], axis=AX),
         reads=["psS1"], writes=["svals"])
    S.op("v", lambda e: e.reduce_sum(svals[:, 1:2], psS1[:, NWT_FC:NWT_FC + NWT_PR], axis=AX),
         reads=["psS1"], writes=["svals"])
    inv = 1.0 / float(D * H)
    S.op("v", lambda e: e.tensor_scalar(svals[:, 2:3], svals[:, 0:1], inv, None, ALU.mult),
         reads=["svals"], writes=["svals"])
    S.op("v", lambda e: e.tensor_scalar(svals[:, 3:4], svals[:, 1:2], inv, None, ALU.mult),
         reads=["svals"], writes=["svals"])
    S.op("v", lambda e: e.tensor_scalar(svals[:, 4:5], svals[:, 2:3], 0.7, None, ALU.mult),
         reads=["svals"], writes=["svals"])
    S.op("v", lambda e: e.tensor_scalar(svals[:, 5:6], svals[:, 3:4], 0.7, None, ALU.mult),
         reads=["svals"], writes=["svals"])
    S.op("v", lambda e: e.tensor_mul(svals[:, 6:7], svals[:, 2:3], svals[:, 2:3]),
         reads=["svals"], writes=["svals"])
    S.op("v", lambda e: e.tensor_mul(svals[:, 6:7], svals[:, 6:7], svals[:, 3:4]),
         reads=["svals"], writes=["svals"])
    S.op("v", lambda e: e.tensor_scalar(svals[:, 6:7], svals[:, 6:7], 0.5, None, ALU.mult),
         reads=["svals"], writes=["svals"])
    S.op("v", lambda e: e.tensor_copy(sm[:, 0:1], svals[:, 4:5]), reads=["svals"], writes=["sm"])
    S.op("v", lambda e: e.tensor_copy(sm[:, 1:2], svals[:, 5:6]), reads=["svals"], writes=["sm"])
    S.op("v", lambda e: e.tensor_copy(sm[:, 2:3], svals[:, 6:7]), reads=["svals"], writes=["sm"])
    S.op("v", lambda e: e.tensor_scalar(sm[:, 3:4], svals[:, 4:5], -1.0, None, ALU.mult),
         reads=["svals"], writes=["sm"])
    S.op("v", lambda e: e.tensor_scalar(sm[:, 4:5], svals[:, 5:6], -1.0, None, ALU.mult),
         reads=["svals"], writes=["sm"])
    S.group("t", [lambda e: e.matmul(psC1[:, 0:6], oner[:], sm[:, 0:6], start=True, stop=True)],
            reads=["sm", "oner"], writes=["psC1"])
    S.op("v", lambda e: e.tensor_copy(thrv[:], psC1[:, 0:6]), reads=["psC1"], writes=["thrv"])

    for i in range(NWT_FC + NWT_PR if upto >= 2 else 0):
        fc_part = i < NWT_FC
        src = fcf[i] if fc_part else prf[i - NWT_FC]
        dst = qfcf[i] if fc_part else qprf[i - NWT_FC]
        thr_col = thrv[:, 0:1] if fc_part else thrv[:, 1:2]
        nthr_col = thrv[:, 3:4] if fc_part else thrv[:, 4:5]
        bi = i % 2
        buf_key = "wA" if bi == 0 else "wB"
        buf = pool[:, bi * 1024:(bi + 1) * 1024]
        qv = qb[:, bi * 1024:(bi + 1) * 1024]
        mv = mk[:, bi * 1024:(bi + 1) * 1024]
        qk, mkk = f"qb{bi}", f"mk{bi}"
        S.dma("y", buf, src, writes=[buf_key], fam="wa" if bi == 0 else "wb")
        S.op("v", lambda e, o=qv, i_=buf, t_=thr_col: e.tensor_scalar(
            o, i_, t_, None, ALU.is_gt), reads=[buf_key, "thrv"], writes=[qk])
        S.op("g", lambda e, o=mv, i_=buf, t_=nthr_col: e.tensor_scalar(
            o, i_, t_, None, ALU.is_lt), reads=[buf_key, "thrv"], writes=[mkk])
        S.op("v", lambda e, o=qv, a=qv, b=mv: e.tensor_sub(o, a, b),
             reads=[qk, mkk], writes=[qk])
        scr_tags = []
        if i in (NWT_FC - 2, NWT_FC - 1):
            scr_tags = ["qfc_scrA" if i % 2 == 0 else "qfc_scrB"]
        if i in (NWT_FC + NWT_PR - 2, NWT_FC + NWT_PR - 1):
            scr_tags = ["qpr_scrA" if i % 2 == 0 else "qpr_scrB"]
        S.dma("s", dst, qv, reads=[qk], writes=scr_tags,
              fam=("qf" if fc_part else "qp") + str(bi))

    ws_state = {"idx": 0}

    def next_slot():
        i = ws_state["idx"] % 3
        ws_state["idx"] += 1
        return ws[:, i * 2560:(i + 1) * 2560], f"ws{i}"

    SCR_TAGS = {"qfc_scr": ["qfc_scrA", "qfc_scrB"], "qpr_scr": ["qpr_scrA", "qpr_scrB"],
                "winq": ["winq"], "woutq": ["woutq"]}

    def load_wset(scr_k, kcount, mcol, slot, skey, scr_key, width=128):
        rtags = SCR_TAGS[scr_key]
        fam = "w" + skey[-1]
        first = True
        for k0 in range(0, kcount, 4):
            kk = min(4, kcount - k0)
            src = scr_k[k0:k0 + kk, :, mcol:mcol + width].rearrange("kk p f -> p kk f")
            dst = slot[:, k0 * width:(k0 + kk) * width].rearrange("p (kk f) -> p kk f", kk=kk)
            S.dma("y", dst, src, reads=rtags, writes=[skey] if first else [], fam=fam)
            first = False
        S.writer[skey] = (("f", fam), S.cnt[("f", fam)] * 16)

    def ln_smalls(nfeat, ts, mu_dst=None, rs_dst=None):
        iv = 1.0 / float(nfeat)
        S.op("v", lambda e: e.tensor_scalar(sm[:, 0:ts], psS1[:, 0:ts], iv, None, ALU.mult),
             reads=["psS1"], writes=["sm"])
        S.op("v", lambda e: e.tensor_scalar(sm2[:, 0:ts], psS2[:, 0:ts], iv, None, ALU.mult),
             reads=["psS2"], writes=["sm2"])
        S.op("v", lambda e: e.tensor_mul(sm3[:, 0:ts], sm[:, 0:ts], sm[:, 0:ts]),
             reads=["sm"], writes=["sm3"])
        S.op("v", lambda e: e.tensor_sub(sm2[:, 0:ts], sm2[:, 0:ts], sm3[:, 0:ts]),
             reads=["sm2", "sm3"], writes=["sm2"])
        S.op("v", lambda e: e.tensor_scalar(sm2[:, 0:ts], sm2[:, 0:ts], LN_EPS, None, ALU.add),
             reads=["sm2"], writes=["sm2"])
        S.op("s", lambda e: e.activation(sm3[:, 0:ts], sm2[:, 0:ts], AF.Ln),
             reads=["sm2"], writes=["sm3"])
        S.op("s", lambda e: e.activation(sm2[:, 0:ts], sm3[:, 0:ts], AF.Exp, scale=-0.5),
             reads=["sm3"], writes=["sm2"])
        if mu_dst is not None:
            S.op("v", lambda e: e.tensor_copy(musv[:, mu_dst[0]:mu_dst[1]], sm[:, 0:ts]),
                 reads=["sm"], writes=["musv"])
            S.op("v", lambda e: e.tensor_copy(rssv[:, mu_dst[0]:mu_dst[1]], sm2[:, 0:ts]),
                 reads=["sm2"], writes=["rssv"])
        S.op("v", lambda e: e.tensor_mul(sm[:, 0:ts], sm[:, 0:ts], sm2[:, 0:ts]),
             reads=["sm", "sm2"], writes=["sm"])
        S.op("v", lambda e: e.tensor_scalar(sm[:, 0:ts], sm[:, 0:ts], -1.0, None, ALU.mult),
             reads=["sm"], writes=["sm"])
        bcast(ts)

    def bcast(ts):
        rs = sm2[:, 0:ts]
        nm = sm[:, 0:ts]
        S.group("t", [lambda e: e.matmul(psC1[:, 0:ts], oner[:], rs, start=True, stop=True)],
                reads=["sm2", "oner"], writes=["psC1"])
        S.group("t", [lambda e: e.matmul(psC2[:, 0:ts], oner[:], nm, start=True, stop=True)],
                reads=["sm", "oner"], writes=["psC2"])
        S.op("s", lambda e: e.copy(C1[:, 0:ts], psC1[:, 0:ts]), reads=["psC1"], writes=["C1"])
        S.op("s", lambda e: e.copy(C2[:, 0:ts], psC2[:, 0:ts]), reads=["psC2"], writes=["C2"])

    def stats(buf, bkeys, ts, cast_from=None, from_keys=None):
        for k in range(K16):
            bap, bkey = buf(k)
            if cast_from is not None:
                sap, skey = cast_from(k)
                S.op("s", lambda e, o=bap, i_=sap: e.copy(o, i_), reads=[skey], writes=[bkey])
            sqv = sq2[:, (k % 2) * 512:(k % 2) * 512 + ts]
            sqk = f"sq{k % 2}"
            S.op("g", lambda e, o=sqv, a=bap: e.tensor_mul(o, a, a), reads=[bkey], writes=[sqk])
            S.group("t", [lambda e, r=bap, k=k: e.matmul(
                psS1[:, 0:ts], onec[:], r, start=(k == 0), stop=(k == K16 - 1))],
                reads=[bkey, "onec"], writes=["psS1"])
            S.group("t", [lambda e, r=sqv, k=k: e.matmul(
                psS2[:, 0:ts], onec[:], r, start=(k == 0), stop=(k == K16 - 1))],
                reads=[sqk, "onec"], writes=["psS2"])

    def normalize(src, skeys, dst, dkeys, ts, gamma_beta=False):
        for k in range(K16):
            sap, skey = src(k)
            dap, dkey = dst(k)
            tv = tch[:, 0:ts]
            lv = l1t[:, 0:ts]
            tk, lk = "tch0", "l1t0"
            S.op("v", lambda e, o=tv, a=sap: e.tensor_mul(o, a, C1[:, 0:ts]),
                 reads=[skey, "C1"], writes=[tk])
            if gamma_beta:
                S.op("v", lambda e, o=lv, t_=tv: e.tensor_add(o, t_, C2[:, 0:ts]),
                     reads=[tk, "C2"], writes=[lk])
                S.op("v", lambda e, o=dap, l_=lv, k=k: e.tensor_scalar(
                    o, l_, gam[:, k:k + 1], bet[:, k:k + 1], ALU.mult, ALU.add),
                    reads=[lk, "gam", "bet"], writes=[dkey])
            else:
                S.op("v", lambda e, o=dap, t_=tv: e.tensor_add(o, t_, C2[:, 0:ts]),
                     reads=[tk, "C2"], writes=[dkey])

    def ffn(t0, ts, src_resid_key, extra_first_writes=(), do_proj=True):
        for m in range(M40):
            slot, skey = next_slot()
            load_wset(qfck, K16, m * 128, slot, skey, "qfc_scr")
            ps, pk = psG[m % 4]
            fns = [lambda e, k=k, sl=slot, p=ps, t0=t0, ts=ts: e.matmul(
                p[:, 0:ts], sl[:, k * 128:(k + 1) * 128],
                lnb[:, k * 512:k * 512 + ts], start=(k == 0), stop=(k == K16 - 1))
                for k in range(K16)]
            S.group("t", fns, reads=[skey] + [f"lnb{k}" for k in range(K16)], writes=[pk])
            sqv = sq2[:, (m % 2) * 512:(m % 2) * 512 + ts]
            sqk = f"sq{m % 2}"
            S.op("s", lambda e, o=sqv, p=ps: e.activation(o, p[:, 0:ts], AF.Relu),
                 reads=[pk], writes=[sqk])
            S.op("g", lambda e, o=hb[:, m * 512:m * 512 + ts], a=sqv: e.tensor_mul(o, a, a),
                 reads=[sqk], writes=[f"hb{m}"])
        for m2 in range(K16 if do_proj else 0):
            slotA, skA = next_slot()
            slotB, skB = next_slot()
            load_wset(qprk, 20, m2 * 128, slotA, skA, "qpr_scr")
            load_wset(qprk[20:40], 20, m2 * 128, slotB, skB, "qpr_scr")
            ps, pk = psG[m2 % 4]
            fns = []
            for k in range(M40):
                sl = slotA if k < 20 else slotB
                kk = k if k < 20 else k - 20
                fns.append(lambda e, k=k, kk=kk, sl=sl, p=ps: e.matmul(
                    p[:, 0:ts], sl[:, kk * 128:(kk + 1) * 128],
                    hb[:, k * 512:k * 512 + ts], start=(k == 0), stop=(k == M40 - 1)))
            S.group("t", fns, reads=[skA, skB] + [f"hb{k}" for k in range(M40)], writes=[pk])
            rv = resid[:, m2 * 512:m2 * 512 + ts]
            S.op("v", lambda e, o=rv, p=ps: e.scalar_tensor_tensor(
                o, p[:, 0:ts], thrv[:, 2:3], o, ALU.mult, ALU.add),
                reads=[pk, "thrv", f"res{m2}"], writes=[f"res{m2}"])

    def emit_phase1():
        for ti, (t0, ts) in enumerate(TILES):
            tile1(ti, t0, ts)

    def tile1(ti, t0, ts):
        if True:
            for k in range(K16):
                S.dma("s", resid[:, k * 512:k * 512 + ts], xT[k * 128:(k + 1) * 128, t0:t0 + ts],
                      writes=[f"res{k}"], fam=f"r{k}")
            stats(lambda k: (lnb[:, k * 512:k * 512 + ts], f"lnb{k}"), None, ts,
                  cast_from=lambda k: (resid[:, k * 512:k * 512 + ts], f"res{k}"))
            if SUB < 2:
                return
            ln_smalls(D, ts)
            normalize(lambda k: (resid[:, k * 512:k * 512 + ts], f"res{k}"), None,
                      lambda k: (lnb[:, k * 512:k * 512 + ts], f"lnb{k}"), None, ts)
            if SUB < 3:
                return
            ffn(t0, ts, "res", do_proj=(SUB >= 4))
            if SUB < 5:
                return
            for k in range(K16):
                S.dma("s", y1s[k * 128:(k + 1) * 128, t0:t0 + ts],
                      resid[:, k * 512:k * 512 + ts], reads=[f"res{k}"], writes=[f"y1s{k}"],
                      fam=f"r{k}")
            if SUB < 6:
                return
            stats(lambda k: (lnb[:, k * 512:k * 512 + ts], f"lnb{k}"), None, ts,
                  cast_from=lambda k: (resid[:, k * 512:k * 512 + ts], f"res{k}"))
            ln_smalls(D, ts, mu_dst=(t0, t0 + ts))
            if SUB < 7:
                return
            normalize(lambda k: (resid[:, k * 512:k * 512 + ts], f"res{k}"), None,
                      lambda k: (lnb[:, k * 512:k * 512 + ts], f"lnb{k}"), None, ts)
            if SUB < 8:
                return
            for m in range(2):
                slot, skey = next_slot()
                load_wset(winqk, K16, m * 128, slot, skey, "winq")
                if SUB < 9:
                    continue
                ps, pk = (psA, "psA") if m == 0 else (psB, "psB")
                fns = [lambda e, k=k, sl=slot, p=ps: e.matmul(
                    p[:, 0:ts], sl[:, k * 128:(k + 1) * 128],
                    lnb[:, k * 512:k * 512 + ts], start=(k == 0), stop=(k == K16 - 1))
                    for k in range(K16)]
                S.group("t", fns, reads=[skey] + [f"lnb{k}" for k in range(K16)], writes=[pk])
                if SUB < 10:
                    continue
                S.op("s", lambda e, o=zbuf[:, m * TOK + t0:m * TOK + t0 + ts], p=ps:
                     e.copy(o, p[:, 0:ts]), reads=[pk], writes=[f"zb{m}"])
                if SUB < 11:
                    continue
                S.op("s", lambda e, o=pool[:, 4608 + m * TOK + t0:4608 + m * TOK + t0 + ts],
                     p=ps: e.copy(o, p[:, 0:ts]), reads=[pk], writes=[f"zf{m}"])

    def emit_phase2():
        for d, order in ((1, list(reversed(range(len(TILES))))), (0, list(range(len(TILES))))):
            for c in range(2):
                for ti in order:
                    t0, ts = TILES[ti]
                    scan_tile(d, c, ti, t0, ts, order)

    def scan_tile(d, c, ti, t0, ts, order):
        if True:
            if True:
                if True:
                    fns = [lambda e, k2=k2: e.matmul(
                        psA[:, 0:ts], wfb[:, d * 1024 + k2 * 512 + c * 128:
                                          d * 1024 + k2 * 512 + c * 128 + 128],
                        zbuf[:, k2 * TOK + t0:k2 * TOK + t0 + ts],
                        start=(k2 == 0), stop=(k2 == 1)) for k2 in range(2)]
                    S.group("t", fns, reads=["wfb", "zb0", "zb1"], writes=["psA"])
                    S.op("s", lambda e: e.activation(dg[:, 0:ts], psA[:, 0:ts], AF.Sigmoid),
                         reads=["psA"], writes=["dgd"])
                    fns = [lambda e, k2=k2: e.matmul(
                        psB[:, 0:ts], wfb[:, d * 1024 + k2 * 512 + 256 + c * 128:
                                          d * 1024 + k2 * 512 + 256 + c * 128 + 128],
                        zbuf[:, k2 * TOK + t0:k2 * TOK + t0 + ts],
                        start=(k2 == 0), stop=(k2 == 1)) for k2 in range(2)]
                    S.group("t", fns, reads=["wfb", "zb0", "zb1"], writes=["psB"])
                    S.op("v", lambda e: e.tensor_mul(
                        dg[:, 512:512 + ts], psB[:, 0:ts],
                        pool[:, 4608 + c * TOK + t0:4608 + c * TOK + t0 + ts]),
                        reads=["psB", f"zf{c}"], writes=["dgg"])
                    hk = f"hsc{c}"
                    hview = pool[:, c * TOK + t0:c * TOK + t0 + ts]
                    d0 = dg[:, 0:ts]
                    d1 = dg[:, 512:512 + ts]
                    extra_w = ["wA", "wB"] if (ti == order[0] and c == 0) else []
                    if d == 0:
                        init = 0.0 if ti == 0 else pool[:, c * TOK + t0 - 1:c * TOK + t0]
                        S.op("v", lambda e, o=hview, a=d0, b=d1, i_=init: e.tensor_tensor_scan(
                            o, a, b, i_, ALU.mult, ALU.add),
                            reads=["dgd", "dgg", hk], writes=[hk] + extra_w)
                    else:
                        last = ti == len(TILES) - 1
                        init = 0.0 if last else pool[:, c * TOK + t0 + ts:c * TOK + t0 + ts + 1]
                        S.op("v", lambda e, o=hview[:, ::-1], a=d0[:, ::-1], b=d1[:, ::-1],
                             i_=init: e.tensor_tensor_scan(o, a, b, i_, ALU.mult, ALU.add),
                            reads=["dgd", "dgg", hk], writes=[hk] + extra_w)
                    hcol = (c if d == 0 else 2 + c) * TOK + t0
                    wtags = [f"hfb{c if d == 0 else 2 + c}"]
                    if d == 1:
                        wtags += ["qb0", "qb1", "mk0", "mk1"]
                    S.op("s", lambda e, o=hfb[:, hcol:hcol + ts], i_=hview: e.copy(o, i_),
                         reads=[hk], writes=wtags)

    def emit_phase3():
        for ti, (t0, ts) in enumerate(TILES):
            tile3(ti, t0, ts)

    def tile3(ti, t0, ts):
        if True:
            for k in range(K16):
                S.dma("s", resid[:, k * 512:k * 512 + ts], y1s[k * 128:(k + 1) * 128, t0:t0 + ts],
                      reads=[f"y1s{k}"], writes=[f"res{k}"], fam=f"r{k}")
            S.op("v", lambda e: e.tensor_copy(sm2[:, 0:ts], rssv[:, t0:t0 + ts]),
                 reads=["rssv"], writes=["sm2"])
            S.op("v", lambda e: e.tensor_mul(sm[:, 0:ts], musv[:, t0:t0 + ts], sm2[:, 0:ts]),
                 reads=["musv", "sm2"], writes=["sm"])
            S.op("v", lambda e: e.tensor_scalar(sm[:, 0:ts], sm[:, 0:ts], -1.0, None, ALU.mult),
                 reads=["sm"], writes=["sm"])
            bcast(ts)
            for m in range(K16):
                slot, skey = next_slot()
                load_wset(woutqk, 4, m * 128, slot, skey, "woutq")
                ps, pk = psG[m % 4]
                fns = [lambda e, k=k, sl=slot, p=ps: e.matmul(
                    p[:, 0:ts], sl[:, k * 128:(k + 1) * 128],
                    hfb[:, k * TOK + t0:k * TOK + t0 + ts], start=(k == 0), stop=(k == 3))
                    for k in range(4)]
                S.group("t", fns, reads=[skey, "hfb0", "hfb1", "hfb2", "hfb3"], writes=[pk])
                rv = resid[:, m * 512:m * 512 + ts]
                tv = tch[:, 0:ts]
                lv = l1t[:, 0:ts]
                tk, lk = "tch0", "l1t0"
                S.op("v", lambda e, a=rv, t_=tv: e.tensor_mul(t_, a, C1[:, 0:ts]),
                     reads=[f"res{m}", "C1"], writes=[tk])
                S.op("v", lambda e, t_=tv, l_=lv: e.tensor_add(l_, t_, C2[:, 0:ts]),
                     reads=[tk, "C2"], writes=[lk])
                extra = ["zf0", "zf1", "hsc0", "hsc1"] if m == 0 else []
                S.op("v", lambda e, o=pool[:, m * 512:m * 512 + ts], p=ps, l_=lv: e.tensor_add(
                    o, p[:, 0:ts], l_),
                    reads=[pk, lk], writes=[f"R2_{m}"] + extra)
            stats(lambda k: (lnb[:, k * 512:k * 512 + ts], f"lnb{k}"), None, ts,
                  cast_from=lambda k: (pool[:, k * 512:k * 512 + ts], f"R2_{k}"))
            ln_smalls(D, ts)
            for k in range(K16):
                pv = pool[:, k * 512:k * 512 + ts]
                tv = tch[:, 0:ts]
                lv = l1t[:, 0:ts]
                tk, lk = "tch0", "l1t0"
                S.op("v", lambda e, a=pv, t_=tv: e.tensor_mul(t_, a, C1[:, 0:ts]),
                     reads=[f"R2_{k}", "C1"], writes=[tk])
                S.op("v", lambda e, t_=tv, l_=lv: e.tensor_add(l_, t_, C2[:, 0:ts]),
                     reads=[tk, "C2"], writes=[lk])
                S.op("v", lambda e, k=k, l_=lv: e.tensor_scalar(
                    l_, l_, gam[:, k:k + 1], bet[:, k:k + 1],
                    ALU.mult, ALU.add), reads=[lk, "gam", "bet"], writes=[lk])
                rv = resid[:, k * 512:k * 512 + ts]
                S.op("v", lambda e, o=rv, l_=lv: e.tensor_add(o, o, l_),
                     reads=[f"res{k}", lk], writes=[f"res{k}"])
            stats(lambda k: (lnb[:, k * 512:k * 512 + ts], f"lnb{k}"), None, ts,
                  cast_from=lambda k: (resid[:, k * 512:k * 512 + ts], f"res{k}"))
            ln_smalls(D, ts)
            normalize(lambda k: (resid[:, k * 512:k * 512 + ts], f"res{k}"), None,
                      lambda k: (lnb[:, k * 512:k * 512 + ts], f"lnb{k}"), None, ts)
            ffn(t0, ts, "res")
            o0 = max(t0, OWN0)
            o1 = min(t0 + ts, OWN0 + OWN)
            if o1 > o0:
                for k in range(K16):
                    S.dma("s", yT[k * 128:(k + 1) * 128, o0 - OWN0:o1 - OWN0],
                          resid[:, k * 512 + (o0 - t0):k * 512 + (o1 - t0)],
                          reads=[f"res{k}"], fam="out")

    if upto >= 3:
        emit_phase1()
    if upto >= 4:
        emit_phase2()
    if upto >= 5:
        emit_phase3()

    for fam, sem in S.fsem.items():
        v = S.cnt[("f", fam)] * 16
        S.ops["s"].append(lambda e, sem=sem, v=v: e.wait_ge(sem, v))

    with es:
        @block.sync
        def _(e):
            for f in S.ops["y"]:
                f(e)

        @block.vector
        def _(e):
            for f in S.ops["v"]:
                f(e)

        @block.scalar
        def _(e):
            for f in S.ops["s"]:
                f(e)

        @block.gpsimd
        def _(e):
            for f in S.ops["g"]:
                f(e)

        @block.tensor
        def _(e):
            for f in S.ops["t"]:
                f(e)

    return nc


_NC = None


def kernel(L0_attn_x, L0_attn_W_in, L0_attn_W_fwd, L0_attn_W_bwd, L0_attn_W_out,
           L0_attn_gamma, L0_attn_beta, L0_ffn_w_fc, L0_ffn_w_proj):
    global _NC
    x = np.asarray(L0_attn_x, dtype=np.float32)
    B, L, Dm = x.shape
    wfcT = np.ascontiguousarray(np.asarray(L0_ffn_w_fc, np.float32).T)
    wprojT = np.ascontiguousarray(np.asarray(L0_ffn_w_proj, np.float32).T)
    win = np.ascontiguousarray(np.asarray(L0_attn_W_in, np.float32))
    wfwd = np.ascontiguousarray(np.asarray(L0_attn_W_fwd, np.float32))
    wbwd = np.ascontiguousarray(np.asarray(L0_attn_W_bwd, np.float32))
    wout = np.ascontiguousarray(np.asarray(L0_attn_W_out, np.float32))
    gamT = np.ascontiguousarray(np.asarray(L0_attn_gamma, np.float32).reshape(K16, 128).T)
    betT = np.ascontiguousarray(np.asarray(L0_attn_beta, np.float32).reshape(K16, 128).T)

    if _NC is None:
        nc = bass.Bass("TRN2", target_bir_lowering=False)
        _NC = build(nc)

    in_maps = []
    for c in range(8):
        b, s = c // 2, c % 2
        pad = np.zeros((TOK, Dm), np.float32)
        lo = s * OWN - OWN0
        hi = s * OWN + OWN + OWN0
        clo, chi = max(lo, 0), min(hi, L)
        pad[clo - lo:chi - lo, :] = x[b, clo:chi, :]
        xT = np.ascontiguousarray(pad.T)
        in_maps.append(dict(xT=xT, wfcT=wfcT, wprojT=wprojT, win=win, wfwd=wfwd,
                            wbwd=wbwd, wout=wout, gamT=gamT, betT=betT))

    res = run_bass_kernel_spmd(_NC, in_maps, core_ids=list(range(8)))
    out = np.empty((B, L, Dm), np.float32)
    for c in range(8):
        b, s = c // 2, c % 2
        out[b, s * OWN:(s + 1) * OWN, :] = np.asarray(res.results[c]["yT"]).T
    return out


# version 06
# speedup vs baseline: 1.4796x; 1.1841x over previous
import numpy as np
import concourse.bass as bass
import concourse.mybir as mybir
from concourse.bass_utils import run_bass_kernel_spmd
from contextlib import ExitStack

F32 = mybir.dt.float32
BF16 = mybir.dt.bfloat16
AF = mybir.ActivationFunctionType
ALU = mybir.AluOpType
AX = mybir.AxisListType.X

D = 2048
H = 5120
DI = 256
TOK = 2176
OWN0 = 64
OWN = 2048
K16 = 16
M40 = 40
TILES = [(0, 512), (512, 512), (1024, 512), (1536, 512), (2048, 128)]
NWT_FC = 80
NWT_PR = 80
LN_EPS = 1e-5

STREAMS = ("t", "v", "s", "g", "y")


class Sched:
    def __init__(self, nc, es):
        self.nc = nc
        self.es = es
        self.ops = {e: [] for e in STREAMS}
        self.csem = {e: es.enter_context(nc.semaphore("c_" + e)) for e in STREAMS}
        self.fsem = {}
        self.cnt = {("c", e): 0 for e in STREAMS}
        self.waited = {e: {} for e in STREAMS}
        self.writer = {}
        self.readers = {}

    def _sem(self, key):
        kind, e = key
        return self.csem[e] if kind == "c" else self.fsem[e]

    def _wait(self, eng, mkey, val):
        if val <= 0:
            return
        if self.waited[eng].get(mkey, 0) >= val:
            return
        self.waited[eng][mkey] = val
        sem = self._sem(mkey)
        self.ops[eng].append(lambda e, sem=sem, val=val: e.wait_ge(sem, val))

    def _deps(self, eng, reads, writes):
        for r in reads:
            w = self.writer.get(r)
            if w:
                self._wait(eng, w[0], w[1])
        for wr in writes:
            w = self.writer.get(wr)
            if w:
                self._wait(eng, w[0], w[1])
            for rd in self.readers.get(wr, ()):
                self._wait(eng, rd[0], rd[1])

    def _record(self, my, reads, writes):
        for r in reads:
            self.readers.setdefault(r, []).append(my)
        for wr in writes:
            self.writer[wr] = my
            self.readers[wr] = []

    def op(self, eng, fn, reads=(), writes=()):
        self._deps(eng, reads, writes)
        key = ("c", eng)
        self.cnt[key] += 1
        sem, val = self.csem[eng], self.cnt[key]
        self.ops[eng].append(lambda e, fn=fn, sem=sem: fn(e).then_inc(sem, 1))
        self._record((key, val), reads, writes)

    def group(self, eng, fns, reads=(), writes=()):
        self._deps(eng, reads, writes)
        key = ("c", eng)
        self.cnt[key] += 1
        sem, val = self.csem[eng], self.cnt[key]
        for fn in fns[:-1]:
            self.ops[eng].append(lambda e, fn=fn: fn(e))
        self.ops[eng].append(lambda e, fn=fns[-1], sem=sem: fn(e).then_inc(sem, 1))
        self._record((key, val), reads, writes)

    def dma(self, eng, out_ap, in_ap, reads=(), writes=(), fam="misc"):
        self._deps(eng, reads, writes)
        if fam not in self.fsem:
            self.fsem[fam] = self.es.enter_context(self.nc.semaphore("f_" + fam))
            self.cnt[("f", fam)] = 0
        key = ("f", fam)
        self.cnt[key] += 1
        sem, val = self.fsem[fam], self.cnt[key] * 16
        self.ops[eng].append(
            lambda e, o=out_ap, i=in_ap, sem=sem: e.dma_start(out=o, in_=i).then_inc(sem, 16))
        self._record((key, val), reads, writes)


SUB = 99
def build(nc, upto=5):
    es = ExitStack()
    xT = nc.dram_tensor("xT", [D, TOK], F32, kind="ExternalInput")
    wfcT = nc.dram_tensor("wfcT", [D, H], F32, kind="ExternalInput")
    wprojT = nc.dram_tensor("wprojT", [H, D], F32, kind="ExternalInput")
    win = nc.dram_tensor("win", [D, DI], F32, kind="ExternalInput")
    wfwd = nc.dram_tensor("wfwd", [DI, 2 * DI], F32, kind="ExternalInput")
    wbwd = nc.dram_tensor("wbwd", [DI, 2 * DI], F32, kind="ExternalInput")
    wout = nc.dram_tensor("wout", [2 * DI, D], F32, kind="ExternalInput")
    gamT = nc.dram_tensor("gamT", [128, K16], F32, kind="ExternalInput")
    betT = nc.dram_tensor("betT", [128, K16], F32, kind="ExternalInput")
    yT = nc.dram_tensor("yT", [D, OWN], F32, kind="ExternalOutput")

    qfc = nc.dram_tensor("qfc", [D, H], BF16)
    qpr = nc.dram_tensor("qpr", [H, D], BF16)
    winq = nc.dram_tensor("winq", [D, DI], BF16)
    woutq = nc.dram_tensor("woutq", [2 * DI, D], BF16)
    y1s = nc.dram_tensor("y1s", [D, TOK], F32)

    fcf = wfcT.ap().rearrange("a b -> (a b)").rearrange("(n p f) -> n p f", p=128, f=1024)
    prf = wprojT.ap().rearrange("a b -> (a b)").rearrange("(n p f) -> n p f", p=128, f=1024)
    qfcf = qfc.ap().rearrange("a b -> (a b)").rearrange("(n p f) -> n p f", p=128, f=1024)
    qprf = qpr.ap().rearrange("a b -> (a b)").rearrange("(n p f) -> n p f", p=128, f=1024)
    qfck = qfc.ap().rearrange("(k p) h -> k p h", p=128)
    qprk = qpr.ap().rearrange("(k p) h -> k p h", p=128)
    winqk = winq.ap().rearrange("(k p) h -> k p h", p=128)
    woutqk = woutq.ap().rearrange("(k p) h -> k p h", p=128)

    sb = lambda n_, s, d: es.enter_context(nc.sbuf_tensor(n_, s, d))
    resid = sb("resid", [128, K16 * 512], F32)
    lnb = sb("lnb", [128, K16 * 512], BF16)
    hb = sb("hb", [128, M40 * 512], BF16)
    pool = sb("pool", [128, 8960], F32)
    ws = sb("ws", [128, 3 * 2560], BF16)
    zbuf = sb("zbuf", [128, 2 * TOK], BF16)
    hfb = sb("hfb", [128, 4 * TOK], BF16)
    dg = sb("dg", [128, 2 * 512], F32)
    qb = hfb[:, 4352:6400]
    mk = hfb[:, 6400:8448]
    wfb = sb("wfb", [128, 4 * 512], BF16)
    sq2 = sb("sq2", [128, 2 * 512], BF16)
    tch = sb("tch", [128, 512], F32)
    l1t = sb("l1t", [128, 512], F32)
    C1 = sb("C1", [128, 512], F32)
    C2 = sb("C2", [128, 512], F32)
    gam = sb("gam", [128, K16], F32)
    bet = sb("bet", [128, K16], F32)
    asum = sb("asum", [128, NWT_FC + NWT_PR], F32)
    musv = sb("musv", [1, TOK], F32)
    rssv = sb("rssv", [1, TOK], F32)
    sm = sb("sm", [1, 512], F32)
    sm2 = sb("sm2", [1, 512], F32)
    sm3 = sb("sm3", [1, 512], F32)
    onec = sb("onec", [128, 1], BF16)
    onecf = sb("onecf", [128, 1], F32)
    oner = sb("oner", [1, 128], F32)
    thrv = sb("thrv", [128, 6], F32)
    svals = sb("svals", [1, 8], F32)

    psA = es.enter_context(nc.psum_tensor("psA", [128, 512], F32))
    psB = es.enter_context(nc.psum_tensor("psB", [128, 512], F32))
    psS1 = es.enter_context(nc.psum_tensor("psS1", [1, 512], F32))
    psS2 = es.enter_context(nc.psum_tensor("psS2", [1, 512], F32))
    psC = es.enter_context(nc.psum_tensor("psC", [128, 512], F32))
    psD = es.enter_context(nc.psum_tensor("psD", [128, 512], F32))
    psC1 = es.enter_context(nc.psum_tensor("psC1", [128, 512], F32))
    psC2 = es.enter_context(nc.psum_tensor("psC2", [128, 512], F32))

    psG = [(psA, "psA"), (psB, "psB"), (psC, "psC"), (psD, "psD")]
    S = Sched(nc, es)
    block = es.enter_context(nc.Block())

    S.op("v", lambda e: e.memset(sm[:], 0.0), writes=["sm"])
    S.op("v", lambda e: e.memset(onec[:], 1.0), writes=["onec"])
    S.op("v", lambda e: e.memset(onecf[:], 1.0), writes=["onecf"])
    S.op("v", lambda e: e.memset(oner[:], 1.0), writes=["oner"])
    S.dma("y", gam[:], gamT[:], writes=["gam"], fam="gld")
    S.dma("y", bet[:], betT[:], writes=["bet"], fam="bld")

    def cast_small(src2d, k, dst_ap, dst_key, ncols):
        wa = pool[:, 0:ncols]
        S.dma("y", wa, src2d[k * 128:(k + 1) * 128, 0:ncols], writes=["wA"], fam="wa")
        S.op("s", lambda e, d=dst_ap, s_=wa: e.copy(d, s_), reads=["wA"], writes=[dst_key])

    for k in range(2):
        cast_small(wfwd.ap(), k, wfb[:, k * 512:(k + 1) * 512], "wfb", 512)
        cast_small(wbwd.ap(), k, wfb[:, 1024 + k * 512:1024 + (k + 1) * 512], "wfb", 512)
    for k in range(K16):
        wa = pool[:, 0:DI]
        S.dma("y", wa, win[k * 128:(k + 1) * 128, 0:DI], writes=["wA"], fam="wa")
        qv = qb[:, 0:DI]
        S.op("s", lambda e, d=qv, s_=wa: e.copy(d, s_), reads=["wA"], writes=["qb0"])
        S.dma("s", winqk[k], qv, reads=["qb0"], writes=["winq"], fam="winw")
    for k in range(4):
        wa = pool[:, 0:1024]
        wa2 = pool[:, 0:1024]
        for half in range(2):
            src = wout[k * 128:(k + 1) * 128, half * 1024:(half + 1) * 1024]
            S.dma("y", wa, src, writes=["wA"], fam="wa")
            qv = qb[:, 0:1024]
            S.op("s", lambda e, d=qv, s_=wa: e.copy(d, s_), reads=["wA"], writes=["qb0"])
            S.dma("s", woutqk[k][:, half * 1024:(half + 1) * 1024], qv,
                  reads=["qb0"], writes=["woutq"], fam="woutw")

    for i in range(NWT_FC + NWT_PR if upto >= 1 else 0):
        src = fcf[i] if i < NWT_FC else prf[i - NWT_FC]
        buf_key = "wA" if i % 2 == 0 else "wB"
        buf = pool[:, 0:1024] if i % 2 == 0 else pool[:, 1024:2048]
        S.dma("y", buf, src, writes=[buf_key], fam="wa" if i % 2 == 0 else "wb")
        eng = "v"
        S.op(eng, lambda e, o=asum[:, i:i + 1], i_=buf: e.reduce_sum(
            o, i_, axis=AX, apply_absolute_value=True),
            reads=[buf_key], writes=["asum"])
    if upto < 1:
        emit_none = True
    S.group("t", [lambda e: e.matmul(psS1[:, 0:NWT_FC + NWT_PR], onecf[:],
                                     asum[:, 0:NWT_FC + NWT_PR], start=True, stop=True)],
            reads=["asum", "onecf"], writes=["psS1"])
    S.op("v", lambda e: e.reduce_sum(svals[:, 0:1], psS1[:, 0:NWT_FC], axis=AX),
         reads=["psS1"], writes=["svals"])
    S.op("v", lambda e: e.reduce_sum(svals[:, 1:2], psS1[:, NWT_FC:NWT_FC + NWT_PR], axis=AX),
         reads=["psS1"], writes=["svals"])
    inv = 1.0 / float(D * H)
    S.op("v", lambda e: e.tensor_scalar(svals[:, 2:3], svals[:, 0:1], inv, None, ALU.mult),
         reads=["svals"], writes=["svals"])
    S.op("v", lambda e: e.tensor_scalar(svals[:, 3:4], svals[:, 1:2], inv, None, ALU.mult),
         reads=["svals"], writes=["svals"])
    S.op("v", lambda e: e.tensor_scalar(svals[:, 4:5], svals[:, 2:3], 0.7, None, ALU.mult),
         reads=["svals"], writes=["svals"])
    S.op("v", lambda e: e.tensor_scalar(svals[:, 5:6], svals[:, 3:4], 0.7, None, ALU.mult),
         reads=["svals"], writes=["svals"])
    S.op("v", lambda e: e.tensor_mul(svals[:, 6:7], svals[:, 2:3], svals[:, 2:3]),
         reads=["svals"], writes=["svals"])
    S.op("v", lambda e: e.tensor_mul(svals[:, 6:7], svals[:, 6:7], svals[:, 3:4]),
         reads=["svals"], writes=["svals"])
    S.op("v", lambda e: e.tensor_scalar(svals[:, 6:7], svals[:, 6:7], 0.5, None, ALU.mult),
         reads=["svals"], writes=["svals"])
    S.op("v", lambda e: e.tensor_copy(sm[:, 0:1], svals[:, 4:5]), reads=["svals"], writes=["sm"])
    S.op("v", lambda e: e.tensor_copy(sm[:, 1:2], svals[:, 5:6]), reads=["svals"], writes=["sm"])
    S.op("v", lambda e: e.tensor_copy(sm[:, 2:3], svals[:, 6:7]), reads=["svals"], writes=["sm"])
    S.op("v", lambda e: e.tensor_scalar(sm[:, 3:4], svals[:, 4:5], -1.0, None, ALU.mult),
         reads=["svals"], writes=["sm"])
    S.op("v", lambda e: e.tensor_scalar(sm[:, 4:5], svals[:, 5:6], -1.0, None, ALU.mult),
         reads=["svals"], writes=["sm"])
    S.group("t", [lambda e: e.matmul(psC1[:, 0:6], oner[:], sm[:, 0:6], start=True, stop=True)],
            reads=["sm", "oner"], writes=["psC1"])
    S.op("v", lambda e: e.tensor_copy(thrv[:], psC1[:, 0:6]), reads=["psC1"], writes=["thrv"])

    for i in range(NWT_FC + NWT_PR if upto >= 2 else 0):
        fc_part = i < NWT_FC
        src = fcf[i] if fc_part else prf[i - NWT_FC]
        dst = qfcf[i] if fc_part else qprf[i - NWT_FC]
        thr_col = thrv[:, 0:1] if fc_part else thrv[:, 1:2]
        nthr_col = thrv[:, 3:4] if fc_part else thrv[:, 4:5]
        bi = i % 2
        buf_key = "wA" if bi == 0 else "wB"
        buf = pool[:, bi * 1024:(bi + 1) * 1024]
        qv = qb[:, bi * 1024:(bi + 1) * 1024]
        mv = mk[:, bi * 1024:(bi + 1) * 1024]
        qk, mkk = f"qb{bi}", f"mk{bi}"
        S.dma("y", buf, src, writes=[buf_key], fam="wa" if bi == 0 else "wb")
        S.op("v", lambda e, o=qv, i_=buf, t_=thr_col: e.tensor_scalar(
            o, i_, t_, None, ALU.is_gt), reads=[buf_key, "thrv"], writes=[qk])
        S.op("g", lambda e, o=mv, i_=buf, t_=nthr_col: e.tensor_scalar(
            o, i_, t_, None, ALU.is_lt), reads=[buf_key, "thrv"], writes=[mkk])
        S.op("v", lambda e, o=qv, a=qv, b=mv: e.tensor_sub(o, a, b),
             reads=[qk, mkk], writes=[qk])
        scr_tags = []
        if i in (NWT_FC - 2, NWT_FC - 1):
            scr_tags = ["qfc_scrA" if i % 2 == 0 else "qfc_scrB"]
        if i in (NWT_FC + NWT_PR - 2, NWT_FC + NWT_PR - 1):
            scr_tags = ["qpr_scrA" if i % 2 == 0 else "qpr_scrB"]
        S.dma("s", dst, qv, reads=[qk], writes=scr_tags,
              fam=("qf" if fc_part else "qp") + str(bi))

    ws_state = {"idx": 0}

    def next_slot():
        i = ws_state["idx"] % 3
        ws_state["idx"] += 1
        return ws[:, i * 2560:(i + 1) * 2560], f"ws{i}"

    SCR_TAGS = {"qfc_scr": ["qfc_scrA", "qfc_scrB"], "qpr_scr": ["qpr_scrA", "qpr_scrB"],
                "winq": ["winq"], "woutq": ["woutq"]}

    def load_wset(scr_k, kcount, mcol, slot, skey, scr_key, width=128):
        rtags = SCR_TAGS[scr_key]
        fam = "w" + skey[-1]
        first = True
        for k0 in range(0, kcount, 4):
            kk = min(4, kcount - k0)
            src = scr_k[k0:k0 + kk, :, mcol:mcol + width].rearrange("kk p f -> p kk f")
            dst = slot[:, k0 * width:(k0 + kk) * width].rearrange("p (kk f) -> p kk f", kk=kk)
            S.dma("y", dst, src, reads=rtags, writes=[skey] if first else [], fam=fam)
            first = False
        S.writer[skey] = (("f", fam), S.cnt[("f", fam)] * 16)

    def ln_smalls(nfeat, ts, mu_dst=None, rs_dst=None):
        iv = 1.0 / float(nfeat)
        S.op("v", lambda e: e.tensor_scalar(sm[:, 0:ts], psS1[:, 0:ts], iv, None, ALU.mult),
             reads=["psS1"], writes=["sm"])
        S.op("v", lambda e: e.tensor_scalar(sm2[:, 0:ts], psS2[:, 0:ts], iv, None, ALU.mult),
             reads=["psS2"], writes=["sm2"])
        S.op("v", lambda e: e.tensor_mul(sm3[:, 0:ts], sm[:, 0:ts], sm[:, 0:ts]),
             reads=["sm"], writes=["sm3"])
        S.op("v", lambda e: e.tensor_sub(sm2[:, 0:ts], sm2[:, 0:ts], sm3[:, 0:ts]),
             reads=["sm2", "sm3"], writes=["sm2"])
        S.op("v", lambda e: e.tensor_scalar(sm2[:, 0:ts], sm2[:, 0:ts], LN_EPS, None, ALU.add),
             reads=["sm2"], writes=["sm2"])
        S.op("s", lambda e: e.activation(sm3[:, 0:ts], sm2[:, 0:ts], AF.Ln),
             reads=["sm2"], writes=["sm3"])
        S.op("s", lambda e: e.activation(sm2[:, 0:ts], sm3[:, 0:ts], AF.Exp, scale=-0.5),
             reads=["sm3"], writes=["sm2"])
        if mu_dst is not None:
            S.op("v", lambda e: e.tensor_copy(musv[:, mu_dst[0]:mu_dst[1]], sm[:, 0:ts]),
                 reads=["sm"], writes=["musv"])
            S.op("v", lambda e: e.tensor_copy(rssv[:, mu_dst[0]:mu_dst[1]], sm2[:, 0:ts]),
                 reads=["sm2"], writes=["rssv"])
        S.op("v", lambda e: e.tensor_mul(sm[:, 0:ts], sm[:, 0:ts], sm2[:, 0:ts]),
             reads=["sm", "sm2"], writes=["sm"])
        S.op("v", lambda e: e.tensor_scalar(sm[:, 0:ts], sm[:, 0:ts], -1.0, None, ALU.mult),
             reads=["sm"], writes=["sm"])
        bcast(ts)

    def bcast(ts):
        rs = sm2[:, 0:ts]
        nm = sm[:, 0:ts]
        S.group("t", [lambda e: e.matmul(psC1[:, 0:ts], oner[:], rs, start=True, stop=True)],
                reads=["sm2", "oner"], writes=["psC1"])
        S.group("t", [lambda e: e.matmul(psC2[:, 0:ts], oner[:], nm, start=True, stop=True)],
                reads=["sm", "oner"], writes=["psC2"])
        S.op("s", lambda e: e.copy(C1[:, 0:ts], psC1[:, 0:ts]), reads=["psC1"], writes=["C1"])
        S.op("s", lambda e: e.copy(C2[:, 0:ts], psC2[:, 0:ts]), reads=["psC2"], writes=["C2"])

    def stats(buf, bkeys, ts, cast_from=None, from_keys=None):
        for k in range(K16):
            bap, bkey = buf(k)
            if cast_from is not None:
                sap, skey = cast_from(k)
                S.op("s", lambda e, o=bap, i_=sap: e.copy(o, i_), reads=[skey], writes=[bkey])
            sqv = sq2[:, (k % 2) * 512:(k % 2) * 512 + ts]
            sqk = f"sq{k % 2}"
            S.op("g", lambda e, o=sqv, a=bap: e.tensor_mul(o, a, a), reads=[bkey], writes=[sqk])
            S.group("t", [lambda e, r=bap, k=k: e.matmul(
                psS1[:, 0:ts], onec[:], r, start=(k == 0), stop=(k == K16 - 1))],
                reads=[bkey, "onec"], writes=["psS1"])
            S.group("t", [lambda e, r=sqv, k=k: e.matmul(
                psS2[:, 0:ts], onec[:], r, start=(k == 0), stop=(k == K16 - 1))],
                reads=[sqk, "onec"], writes=["psS2"])

    def normalize(src, skeys, dst, dkeys, ts, gamma_beta=False):
        for k in range(K16):
            sap, skey = src(k)
            dap, dkey = dst(k)
            tv = tch[:, 0:ts]
            lv = l1t[:, 0:ts]
            tk, lk = "tch0", "l1t0"
            S.op("v", lambda e, o=tv, a=sap: e.tensor_mul(o, a, C1[:, 0:ts]),
                 reads=[skey, "C1"], writes=[tk])
            if gamma_beta:
                S.op("v", lambda e, o=lv, t_=tv: e.tensor_add(o, t_, C2[:, 0:ts]),
                     reads=[tk, "C2"], writes=[lk])
                S.op("v", lambda e, o=dap, l_=lv, k=k: e.tensor_scalar(
                    o, l_, gam[:, k:k + 1], bet[:, k:k + 1], ALU.mult, ALU.add),
                    reads=[lk, "gam", "bet"], writes=[dkey])
            else:
                S.op("v", lambda e, o=dap, t_=tv: e.tensor_add(o, t_, C2[:, 0:ts]),
                     reads=[tk, "C2"], writes=[dkey])

    def ffn(t0, ts, src_resid_key, extra_first_writes=(), do_proj=True):
        for m in range(M40):
            slot, skey = next_slot()
            load_wset(qfck, K16, m * 128, slot, skey, "qfc_scr")
            ps, pk = psG[m % 4]
            fns = [lambda e, k=k, sl=slot, p=ps, t0=t0, ts=ts: e.matmul(
                p[:, 0:ts], sl[:, k * 128:(k + 1) * 128],
                lnb[:, k * 512:k * 512 + ts], start=(k == 0), stop=(k == K16 - 1))
                for k in range(K16)]
            S.group("t", fns, reads=[skey] + [f"lnb{k}" for k in range(K16)], writes=[pk])
            sqv = sq2[:, (m % 2) * 512:(m % 2) * 512 + ts]
            sqk = f"sq{m % 2}"
            S.op("s", lambda e, o=sqv, p=ps: e.activation(o, p[:, 0:ts], AF.Relu),
                 reads=[pk], writes=[sqk])
            S.op("g", lambda e, o=hb[:, m * 512:m * 512 + ts], a=sqv: e.tensor_mul(o, a, a),
                 reads=[sqk], writes=[f"hb{m}"])
        for m2 in range(K16 if do_proj else 0):
            slotA, skA = next_slot()
            slotB, skB = next_slot()
            load_wset(qprk, 20, m2 * 128, slotA, skA, "qpr_scr")
            load_wset(qprk[20:40], 20, m2 * 128, slotB, skB, "qpr_scr")
            ps, pk = psG[m2 % 4]
            fns = []
            for k in range(M40):
                sl = slotA if k < 20 else slotB
                kk = k if k < 20 else k - 20
                fns.append(lambda e, k=k, kk=kk, sl=sl, p=ps: e.matmul(
                    p[:, 0:ts], sl[:, kk * 128:(kk + 1) * 128],
                    hb[:, k * 512:k * 512 + ts], start=(k == 0), stop=(k == M40 - 1)))
            S.group("t", fns, reads=[skA, skB] + [f"hb{k}" for k in range(M40)], writes=[pk])
            rv = resid[:, m2 * 512:m2 * 512 + ts]
            S.op("v", lambda e, o=rv, p=ps: e.scalar_tensor_tensor(
                o, p[:, 0:ts], thrv[:, 2:3], o, ALU.mult, ALU.add),
                reads=[pk, "thrv", f"res{m2}"], writes=[f"res{m2}"])

    def emit_phase1():
        for ti, (t0, ts) in enumerate(TILES):
            tile1(ti, t0, ts)

    def tile1(ti, t0, ts):
        if True:
            for k in range(K16):
                S.dma("s", resid[:, k * 512:k * 512 + ts], xT[k * 128:(k + 1) * 128, t0:t0 + ts],
                      writes=[f"res{k}"], fam=f"r{k}")
            stats(lambda k: (lnb[:, k * 512:k * 512 + ts], f"lnb{k}"), None, ts,
                  cast_from=lambda k: (resid[:, k * 512:k * 512 + ts], f"res{k}"))
            if SUB < 2:
                return
            ln_smalls(D, ts)
            normalize(lambda k: (resid[:, k * 512:k * 512 + ts], f"res{k}"), None,
                      lambda k: (lnb[:, k * 512:k * 512 + ts], f"lnb{k}"), None, ts)
            if SUB < 3:
                return
            ffn(t0, ts, "res", do_proj=(SUB >= 4))
            if SUB < 5:
                return
            for k in range(K16):
                S.dma("s", y1s[k * 128:(k + 1) * 128, t0:t0 + ts],
                      resid[:, k * 512:k * 512 + ts], reads=[f"res{k}"], writes=[f"y1s{k}"],
                      fam=f"r{k}")
            if SUB < 6:
                return
            stats(lambda k: (lnb[:, k * 512:k * 512 + ts], f"lnb{k}"), None, ts,
                  cast_from=lambda k: (resid[:, k * 512:k * 512 + ts], f"res{k}"))
            ln_smalls(D, ts, mu_dst=(t0, t0 + ts))
            if SUB < 7:
                return
            normalize(lambda k: (resid[:, k * 512:k * 512 + ts], f"res{k}"), None,
                      lambda k: (lnb[:, k * 512:k * 512 + ts], f"lnb{k}"), None, ts)
            if SUB < 8:
                return
            for m in range(2):
                slot, skey = next_slot()
                load_wset(winqk, K16, m * 128, slot, skey, "winq")
                if SUB < 9:
                    continue
                ps, pk = (psA, "psA") if m == 0 else (psB, "psB")
                fns = [lambda e, k=k, sl=slot, p=ps: e.matmul(
                    p[:, 0:ts], sl[:, k * 128:(k + 1) * 128],
                    lnb[:, k * 512:k * 512 + ts], start=(k == 0), stop=(k == K16 - 1))
                    for k in range(K16)]
                S.group("t", fns, reads=[skey] + [f"lnb{k}" for k in range(K16)], writes=[pk])
                if SUB < 10:
                    continue
                S.op("s", lambda e, o=zbuf[:, m * TOK + t0:m * TOK + t0 + ts], p=ps:
                     e.copy(o, p[:, 0:ts]), reads=[pk], writes=[f"zb{m}"])
                if SUB < 11:
                    continue
                S.op("s", lambda e, o=pool[:, 4608 + m * TOK + t0:4608 + m * TOK + t0 + ts],
                     p=ps: e.copy(o, p[:, 0:ts]), reads=[pk], writes=[f"zf{m}"])

    def emit_phase2():
        for d, order in ((1, list(reversed(range(len(TILES))))), (0, list(range(len(TILES))))):
            for c in range(2):
                for ti in order:
                    t0, ts = TILES[ti]
                    scan_tile(d, c, ti, t0, ts, order)

    def scan_tile(d, c, ti, t0, ts, order):
        if True:
            if True:
                if True:
                    fns = [lambda e, k2=k2: e.matmul(
                        psA[:, 0:ts], wfb[:, d * 1024 + k2 * 512 + c * 128:
                                          d * 1024 + k2 * 512 + c * 128 + 128],
                        zbuf[:, k2 * TOK + t0:k2 * TOK + t0 + ts],
                        start=(k2 == 0), stop=(k2 == 1)) for k2 in range(2)]
                    S.group("t", fns, reads=["wfb", "zb0", "zb1"], writes=["psA"])
                    S.op("s", lambda e: e.activation(dg[:, 0:ts], psA[:, 0:ts], AF.Sigmoid),
                         reads=["psA"], writes=["dgd"])
                    fns = [lambda e, k2=k2: e.matmul(
                        psB[:, 0:ts], wfb[:, d * 1024 + k2 * 512 + 256 + c * 128:
                                          d * 1024 + k2 * 512 + 256 + c * 128 + 128],
                        zbuf[:, k2 * TOK + t0:k2 * TOK + t0 + ts],
                        start=(k2 == 0), stop=(k2 == 1)) for k2 in range(2)]
                    S.group("t", fns, reads=["wfb", "zb0", "zb1"], writes=["psB"])
                    S.op("v", lambda e: e.tensor_mul(
                        dg[:, 512:512 + ts], psB[:, 0:ts],
                        pool[:, 4608 + c * TOK + t0:4608 + c * TOK + t0 + ts]),
                        reads=["psB", f"zf{c}"], writes=["dgg"])
                    hk = f"hsc{c}"
                    hview = pool[:, c * TOK + t0:c * TOK + t0 + ts]
                    d0 = dg[:, 0:ts]
                    d1 = dg[:, 512:512 + ts]
                    extra_w = ["wA", "wB"] if (ti == order[0] and c == 0) else []
                    if d == 0:
                        init = 0.0 if ti == 0 else pool[:, c * TOK + t0 - 1:c * TOK + t0]
                        S.op("v", lambda e, o=hview, a=d0, b=d1, i_=init: e.tensor_tensor_scan(
                            o, a, b, i_, ALU.mult, ALU.add),
                            reads=["dgd", "dgg", hk], writes=[hk] + extra_w)
                    else:
                        last = ti == len(TILES) - 1
                        init = 0.0 if last else pool[:, c * TOK + t0 + ts:c * TOK + t0 + ts + 1]
                        S.op("v", lambda e, o=hview[:, ::-1], a=d0[:, ::-1], b=d1[:, ::-1],
                             i_=init: e.tensor_tensor_scan(o, a, b, i_, ALU.mult, ALU.add),
                            reads=["dgd", "dgg", hk], writes=[hk] + extra_w)
                    hcol = (c if d == 0 else 2 + c) * TOK + t0
                    wtags = [f"hfb{c if d == 0 else 2 + c}"]
                    if d == 1:
                        wtags += ["qb0", "qb1", "mk0", "mk1"]
                    S.op("s", lambda e, o=hfb[:, hcol:hcol + ts], i_=hview: e.copy(o, i_),
                         reads=[hk], writes=wtags)

    def emit_phase3():
        for ti, (t0, ts) in enumerate(TILES):
            tile3(ti, t0, ts)

    def tile3(ti, t0, ts):
        if True:
            for k in range(K16):
                S.dma("s", resid[:, k * 512:k * 512 + ts], y1s[k * 128:(k + 1) * 128, t0:t0 + ts],
                      reads=[f"y1s{k}"], writes=[f"res{k}"], fam=f"r{k}")
            S.op("v", lambda e: e.tensor_copy(sm2[:, 0:ts], rssv[:, t0:t0 + ts]),
                 reads=["rssv"], writes=["sm2"])
            S.op("v", lambda e: e.tensor_mul(sm[:, 0:ts], musv[:, t0:t0 + ts], sm2[:, 0:ts]),
                 reads=["musv", "sm2"], writes=["sm"])
            S.op("v", lambda e: e.tensor_scalar(sm[:, 0:ts], sm[:, 0:ts], -1.0, None, ALU.mult),
                 reads=["sm"], writes=["sm"])
            bcast(ts)
            for m in range(K16):
                slot, skey = next_slot()
                load_wset(woutqk, 4, m * 128, slot, skey, "woutq")
                ps, pk = psG[m % 4]
                fns = [lambda e, k=k, sl=slot, p=ps: e.matmul(
                    p[:, 0:ts], sl[:, k * 128:(k + 1) * 128],
                    hfb[:, k * TOK + t0:k * TOK + t0 + ts], start=(k == 0), stop=(k == 3))
                    for k in range(4)]
                S.group("t", fns, reads=[skey, "hfb0", "hfb1", "hfb2", "hfb3"], writes=[pk])
                rv = resid[:, m * 512:m * 512 + ts]
                tv = tch[:, 0:ts]
                lv = l1t[:, 0:ts]
                tk, lk = "tch0", "l1t0"
                S.op("v", lambda e, a=rv, t_=tv: e.tensor_mul(t_, a, C1[:, 0:ts]),
                     reads=[f"res{m}", "C1"], writes=[tk])
                S.op("v", lambda e, t_=tv, l_=lv: e.tensor_add(l_, t_, C2[:, 0:ts]),
                     reads=[tk, "C2"], writes=[lk])
                extra = ["zf0", "zf1", "hsc0", "hsc1"] if m == 0 else []
                S.op("v", lambda e, o=pool[:, m * 512:m * 512 + ts], p=ps, l_=lv: e.tensor_add(
                    o, p[:, 0:ts], l_),
                    reads=[pk, lk], writes=[f"R2_{m}"] + extra)
            stats(lambda k: (lnb[:, k * 512:k * 512 + ts], f"lnb{k}"), None, ts,
                  cast_from=lambda k: (pool[:, k * 512:k * 512 + ts], f"R2_{k}"))
            ln_smalls(D, ts)
            for k in range(K16):
                pv = pool[:, k * 512:k * 512 + ts]
                tv = tch[:, 0:ts]
                lv = l1t[:, 0:ts]
                tk, lk = "tch0", "l1t0"
                S.op("v", lambda e, a=pv, t_=tv: e.tensor_mul(t_, a, C1[:, 0:ts]),
                     reads=[f"R2_{k}", "C1"], writes=[tk])
                S.op("v", lambda e, t_=tv, l_=lv: e.tensor_add(l_, t_, C2[:, 0:ts]),
                     reads=[tk, "C2"], writes=[lk])
                S.op("v", lambda e, k=k, l_=lv: e.tensor_scalar(
                    l_, l_, gam[:, k:k + 1], bet[:, k:k + 1],
                    ALU.mult, ALU.add), reads=[lk, "gam", "bet"], writes=[lk])
                rv = resid[:, k * 512:k * 512 + ts]
                S.op("v", lambda e, o=rv, l_=lv: e.tensor_add(o, o, l_),
                     reads=[f"res{k}", lk], writes=[f"res{k}"])
            stats(lambda k: (lnb[:, k * 512:k * 512 + ts], f"lnb{k}"), None, ts,
                  cast_from=lambda k: (resid[:, k * 512:k * 512 + ts], f"res{k}"))
            ln_smalls(D, ts)
            normalize(lambda k: (resid[:, k * 512:k * 512 + ts], f"res{k}"), None,
                      lambda k: (lnb[:, k * 512:k * 512 + ts], f"lnb{k}"), None, ts)
            ffn(t0, ts, "res")
            o0 = max(t0, OWN0)
            o1 = min(t0 + ts, OWN0 + OWN)
            if o1 > o0:
                for k in range(K16):
                    S.dma("s", yT[k * 128:(k + 1) * 128, o0 - OWN0:o1 - OWN0],
                          resid[:, k * 512 + (o0 - t0):k * 512 + (o1 - t0)],
                          reads=[f"res{k}"], fam="out")

    if upto >= 3:
        emit_phase1()
    if upto >= 4:
        emit_phase2()
    if upto >= 5:
        emit_phase3()

    for fam, sem in S.fsem.items():
        v = S.cnt[("f", fam)] * 16
        S.ops["s"].append(lambda e, sem=sem, v=v: e.wait_ge(sem, v))

    with es:
        @block.sync
        def _(e):
            for f in S.ops["y"]:
                f(e)

        @block.vector
        def _(e):
            for f in S.ops["v"]:
                f(e)

        @block.scalar
        def _(e):
            for f in S.ops["s"]:
                f(e)

        @block.gpsimd
        def _(e):
            for f in S.ops["g"]:
                f(e)

        @block.tensor
        def _(e):
            for f in S.ops["t"]:
                f(e)

    return nc


_NC = None


def kernel(L0_attn_x, L0_attn_W_in, L0_attn_W_fwd, L0_attn_W_bwd, L0_attn_W_out,
           L0_attn_gamma, L0_attn_beta, L0_ffn_w_fc, L0_ffn_w_proj):
    global _NC
    x = np.asarray(L0_attn_x, dtype=np.float32)
    B, L, Dm = x.shape
    wfcT = np.ascontiguousarray(np.asarray(L0_ffn_w_fc, np.float32).T)
    wprojT = np.ascontiguousarray(np.asarray(L0_ffn_w_proj, np.float32).T)
    win = np.ascontiguousarray(np.asarray(L0_attn_W_in, np.float32))
    wfwd = np.ascontiguousarray(np.asarray(L0_attn_W_fwd, np.float32))
    wbwd = np.ascontiguousarray(np.asarray(L0_attn_W_bwd, np.float32))
    wout = np.ascontiguousarray(np.asarray(L0_attn_W_out, np.float32))
    gamT = np.ascontiguousarray(np.asarray(L0_attn_gamma, np.float32).reshape(K16, 128).T)
    betT = np.ascontiguousarray(np.asarray(L0_attn_beta, np.float32).reshape(K16, 128).T)

    if _NC is None:
        nc = bass.Bass("TRN2", target_bir_lowering=False)
        _NC = build(nc)

    in_maps = []
    for c in range(8):
        b, s = c // 2, c % 2
        pad = np.zeros((TOK, Dm), np.float32)
        lo = s * OWN - OWN0
        hi = s * OWN + OWN + OWN0
        clo, chi = max(lo, 0), min(hi, L)
        pad[clo - lo:chi - lo, :] = x[b, clo:chi, :]
        xT = np.ascontiguousarray(pad.T)
        in_maps.append(dict(xT=xT, wfcT=wfcT, wprojT=wprojT, win=win, wfwd=wfwd,
                            wbwd=wbwd, wout=wout, gamT=gamT, betT=betT))

    res = run_bass_kernel_spmd(_NC, in_maps, core_ids=list(range(8)))
    out = np.empty((B, L, Dm), np.float32)
    for c in range(8):
        b, s = c // 2, c % 2
        out[b, s * OWN:(s + 1) * OWN, :] = np.asarray(res.results[c]["yT"]).T
    return out
